# version 20
# baseline (speedup 1.0000x reference)
import numpy as np
import concourse.bass as bass
import concourse.mybir as mybir
from concourse.bass_utils import run_bass_kernel_spmd

F32 = mybir.dt.float32
BF16 = mybir.dt.bfloat16
U8 = mybir.dt.uint8
AF = mybir.ActivationFunctionType
ALU = mybir.AluOpType
AX = mybir.AxisListType

NCORES = 8
R = 4096
D = 1024
DC = D // 128
H = 4096
HC = H // 128
TA = 512
NTA = R // TA
RBA = TA // 128
TC = 512
NTC = R // TC
RBC = TC // 128
CSCALE = 2.0 / 4096.0
CLIP_HI = 2048.0


def build_nc(debug=False):
    nc = bass.Bass("TRN2", target_bir_lowering=False)

    x_d = nc.dram_tensor("x", [R, D], F32, kind="ExternalInput").ap()
    wk_d = nc.dram_tensor("Wk", [D, D], F32, kind="ExternalInput").ap()
    wv_d = nc.dram_tensor("Wv", [D, D], F32, kind="ExternalInput").ap()
    wq_d = nc.dram_tensor("Wq", [D, D], F32, kind="ExternalInput").ap()
    wo_d = nc.dram_tensor("Wout", [D, D], F32, kind="ExternalInput").ap()
    w0_d = nc.dram_tensor("W0", [D, D], F32, kind="ExternalInput").ap()
    w1_d = nc.dram_tensor("W1", [D, D], F32, kind="ExternalInput").ap()
    wgd_d = nc.dram_tensor("Wg_decay", [1, D], F32, kind="ExternalInput").ap()
    bgd_d = nc.dram_tensor("bg_decay", [1, 1], F32, kind="ExternalInput").ap()
    wgl_d = nc.dram_tensor("Wg_lr", [1, D], F32, kind="ExternalInput").ap()
    bgl_d = nc.dram_tensor("bg_lr", [1, 1], F32, kind="ExternalInput").ap()
    k0_d = nc.dram_tensor("k0", [2 * D, H], U8, kind="ExternalInput").ap()
    k1_d = nc.dram_tensor("k1", [H, H], U8, kind="ExternalInput").ap()
    k2_d = nc.dram_tensor("k2", [H, D], U8, kind="ExternalInput").ap()
    k2bf_d = nc.dram_tensor("k2bf", [H, D], BF16).ap()
    s0_d = nc.dram_tensor("s0", [1, 1], F32, kind="ExternalInput").ap()
    s1_d = nc.dram_tensor("s1", [1, 1], F32, kind="ExternalInput").ap()
    s2_d = nc.dram_tensor("s2", [1, 1], F32, kind="ExternalInput").ap()
    out_d = nc.dram_tensor("out", [R, D], F32, kind="ExternalOutput").ap()

    GN = 2 * DC * 128 * D + D
    gin_d = nc.dram_tensor("gbuf_in", [GN], F32).ap()
    gout_d = nc.dram_tensor("gbuf_out", [GN], F32).ap()

    def g1t_dr(buf, ec):
        return buf[ec * 128 * D:(ec + 1) * 128 * D].rearrange("(p d) -> p d", p=128)

    def g0t_dr(buf, ec):
        off = DC * 128 * D
        return buf[off + ec * 128 * D: off + (ec + 1) * 128 * D].rearrange(
            "(p d) -> p d", p=128)

    def colsum_dr(buf):
        off = 2 * DC * 128 * D
        return buf[off: off + D].rearrange("(p c) -> p c", p=128)

    dbg = {}
    if debug:
        for name, shape in [
            ("dbg_xn", [128, DC, TA]), ("dbg_k", [128, DC, TA]),
            ("dbg_a1", [128, DC, TA]), ("dbg_d2", [128, DC, TA]),
            ("dbg_d1", [128, DC, TA]), ("dbg_ktm", [128, RBA, D]),
            ("dbg_g1", [DC * 128, D]), ("dbg_g0", [DC * 128, D]),
            ("dbg_cs", [128, DC]), ("dbg_gate", [128, 8]),
            ("dbg_w0n", [128, DC, D]), ("dbg_w1n", [128, DC, D]),
            ("dbg_u1", [128, HC, TC]), ("dbg_u2", [128, HC, TC]),
            ("dbg_q", [128, DC, TC]), ("dbg_h", [128, DC, TC]),
            ("dbg_xa", [128, RBA, D]), ("dbg_stat", [128, 32]),
            ("dbg_xntm", [128, RBA, D]),
        ]:
            dbg[name] = nc.dram_tensor(name, shape, F32, kind="ExternalOutput").ap()

    slot = [nc.alloc_sbuf_tensor(f"wslot{i}", [128, DC, D], BF16) for i in range(5)]
    S_KT, S_VTN, S_0T, S_1N, S_1T = 0, 1, 2, 3, 4
    ident = nc.alloc_sbuf_tensor("ident", [128, 128], BF16)
    ones_r_f = nc.alloc_sbuf_tensor("ones_r_f", [1, 128], F32)
    ones_c_f = nc.alloc_sbuf_tensor("ones_c_f", [128, 1], F32)
    ones_r_b = nc.alloc_sbuf_tensor("ones_r_b", [1, 128], BF16)
    ones_c_b = nc.alloc_sbuf_tensor("ones_c_b", [128, 1], BF16)
    scal = nc.alloc_sbuf_tensor("scal", [128, 16], F32)
    C_1MA, C_NLR, C_NLRC, C_S0X2, C_S1X2, C_S2X2, C_BGD, C_BGL = range(8)
    sc11 = nc.alloc_sbuf_tensor("sc11", [1, 8], F32)
    colsum = nc.alloc_sbuf_tensor("colsum", [128, DC], F32)
    cs_tmp = nc.alloc_sbuf_tensor("cs_tmp", [128, DC], F32)
    wg_sb = nc.alloc_sbuf_tensor("wg_sb", [128, 2 * DC], F32)
    gate_sc = nc.alloc_sbuf_tensor("gate_sc", [128, 24], F32)
    stat = nc.alloc_sbuf_tensor("stat", [128, 32], F32)

    ARENA = 126 * 1024
    arena = nc.alloc_sbuf_tensor("arena", [128, ARENA], U8)
    base = nc.lookup_mloc(arena).addr

    def at(name, shape, dtype, off):
        nbytes = int(np.prod(shape[1:])) * mybir.dt.size(dtype)
        assert off % 32 == 0 and off + nbytes <= ARENA, (name, off, nbytes)
        return nc.alloc_sbuf_tensor_at(name, shape, dtype, offset=base + off)

    o = [0]
    def nxt(n):
        r = o[0]
        o[0] += (n + 31) // 32 * 32
        return r
    xA = at("xA", [128, RBA, D], F32, nxt(RBA * D * 4))
    xn_tmA = at("xn_tmA", [128, RBA, D], BF16, nxt(RBA * D * 2))
    xn_fmA = at("xn_fmA", [128, DC, TA], BF16, nxt(DC * TA * 2))
    p_pre1 = nxt(DC * TA * 4)
    pre1_sb = at("pre1_sb", [128, DC, TA], F32, p_pre1)
    k_fm = at("k_fm", [128, DC, TA], BF16, nxt(DC * TA * 2))
    a1_fm = at("a1_fm", [128, DC, TA], BF16, nxt(DC * TA * 2))
    d2_fm = at("d2_fm", [128, DC, TA], BF16, nxt(DC * TA * 2))
    d1_fm = at("d1_fm", [128, DC, TA], BF16, nxt(DC * TA * 2))
    deriv = at("deriv", [128, DC, TA], BF16, nxt(DC * TA * 2))
    k_tm = at("k_tm", [128, RBA, D], BF16, nxt(RBA * D * 2))
    a1_tm = at("a1_tm", [128, RBA, D], BF16, nxt(RBA * D * 2))
    p_d2tm = nxt(RBA * D * 2)
    d2_tm = at("d2_tm", [128, RBA, D], BF16, p_d2tm)
    d1_tm = at("d1_tm", [128, RBA, D], BF16, nxt(RBA * D * 2))
    assert o[0] <= ARENA, o[0]
    gstage = [at(f"gstage{i}", [128, D], F32, p_pre1 + i * D * 4) for i in range(4)]

    o = [0]
    xC = at("xC", [128, RBC, D], F32, nxt(RBC * D * 4))
    xn_tmC = at("xn_tmC", [128, RBC, D], BF16, nxt(RBC * D * 2))
    xn_fmC = at("xn_fmC", [128, DC, TC], BF16, nxt(DC * TC * 2))
    u1 = at("u1", [128, HC, TC], BF16, nxt(HC * TC * 2))
    p_u2 = nxt(HC * TC * 2)
    u2 = at("u2", [128, HC, TC], BF16, p_u2)
    q_fm = at("q_fm", [128, DC, TC], BF16, nxt(DC * TC * 2))
    s_fm = at("s_fm", [128, DC, TC], BF16, nxt(DC * TC * 2))
    w2s = [at(f"w2s{i}", [128, 2, 512], BF16, nxt(2 * 512 * 2)) for i in range(3)]
    k1u8 = at("k1u8", [128, 4, 512], U8, nxt(4 * 512))
    outst = [at(f"outst{i}", [128, 512], F32, nxt(512 * 4)) for i in range(2)]
    gtmp = at("gtmp", [128, 512], F32, nxt(512 * 4))
    assert o[0] <= ARENA, o[0]
    h_fm = q_fm
    k0s = [at(f"k0s{i}", [128, 2 * DC, 512], BF16, p_u2 + i * 2 * DC * 512 * 2)
           for i in range(2)]
    s3a = nc.lookup_mloc(slot[S_1N]).addr
    k1_s = [nc.alloc_sbuf_tensor_at(f"k1_s{i}", [128, 4, 512], BF16,
                                    offset=s3a + i * 4 * 512 * 2)
            for i in range(4)]
    k1u8b = nc.alloc_sbuf_tensor_at(
        "k1u8b", [128, 4, 512], U8, offset=nc.lookup_mloc(w2s[0]).addr)
    k2st = [at(f"k2st{i}", [128, 2, D], BF16, p_d2tm + i * 2 * D * 2)
            for i in range(2)]
    k0u8s = [nc.alloc_sbuf_tensor_at(f"k0u8{i}", [128, 4, 512], U8,
                                     offset=nc.lookup_mloc(outst[i]).addr)
             for i in range(2)]
    wout_nat = nc.alloc_sbuf_tensor_at("wout_nat", [128, DC, D], BF16,
                                       offset=nc.lookup_mloc(q_fm).addr)

    mm0 = nc.alloc_psum_tensor("mm0", [128, 512], F32)
    mm1 = nc.alloc_psum_tensor("mm1", [128, 512], F32)
    gA = nc.alloc_psum_tensor("gA", [128, 1024], F32)
    gB = nc.alloc_psum_tensor("gB", [128, 1024], F32)
    tp0 = nc.alloc_psum_tensor("tp0", [128, 1024], BF16)
    tp1 = nc.alloc_psum_tensor("tp1", [128, 1024], BF16)
    mmrot = [mm0, mm1]
    tprot = [tp0, tp1]

    sems = {n: nc.alloc_semaphore(name=n) for n in
            ["sx", "sg", "pe", "ve", "se", "gp", "cc",
             "ka0", "ka1", "ka2", "kb0", "kb1", "kb2", "kb3",
             "kc0", "kc1", "kc2", "kc3", "sw", "ku",
             "pl0", "pl1", "ps0", "ps1", "k0a", "k0b"]}
    plan = {e: [] for e in ["tensor", "vector", "scalar", "gpsimd", "sync"]}
    cnt = {n: 0 for n in sems}
    lastwait = {(e, s): -1 for e in plan for s in sems}
    barred = {e: True for e in plan}
    bar_marks = {}

    def tile_barrier():
        for e in plan:
            barred[e] = False
        bar_marks.update({s: cnt[s] for s in ("pe", "ve", "se", "sg", "sx")})

    def emit(eng, fn, waits=(), inc=None, amt=1):
        w = list(waits)
        if not barred[eng]:
            barred[eng] = True
            if eng != "tensor":
                w = w + [(s, bar_marks[s]) for s in ("pe", "ve", "se", "sg", "sx")]
        ww = []
        for (s, v) in w:
            if v > lastwait[(eng, s)]:
                ww.append((s, v))
                lastwait[(eng, s)] = v
        plan[eng].append((ww, fn, inc, amt))
        if inc is not None:
            cnt[inc] += amt
            return cnt[inc]
        return None

    def dma(eng, dst, src, waits=(), sem=None, **kw):
        if sem is None:
            sem = "sg" if eng == "gpsimd" else "sx"
        return emit(eng, lambda e: e.dma_start(out=dst, in_=src, **kw),
                    waits=waits, inc=sem, amt=16)

    evt_free = {}

    def rfree(region):
        return [evt_free[region]] if region in evt_free else []

    def f_const(g):
        g.memset(ident[:, :], 0.0)
        g.affine_select(out=ident[:, :], in_=ident[:, :],
                        compare_op=ALU.not_equal, fill=1.0, base=0,
                        pattern=[[-1, 128]], channel_multiplier=1)
        g.memset(ones_r_f[:, :], 1.0)
        g.memset(ones_r_b[:, :], 1.0)
        g.memset(ones_c_b[:, :], 1.0)
        return g.memset(ones_c_f[:, :], 1.0)
    emit("gpsimd", f_const, inc="gp")

    sx_sc = []
    for i, sd in enumerate([s0_d, s1_d, s2_d, bgd_d, bgl_d]):
        sx_sc.append(dma("sync", sc11[:, i:i + 1], sd[:, :]))
    sx_sc.append(dma("sync", wg_sb[:, 0:DC],
                     wgd_d.rearrange("1 (c p) -> p c", p=128),
                     allow_slow_non_contiguous=True))
    sx_sc.append(dma("sync", wg_sb[:, DC:2 * DC],
                     wgl_d.rearrange("1 (c p) -> p c", p=128),
                     allow_slow_non_contiguous=True))
    sx_scal = sx_sc[-1]

    def f_init_v(v):
        v.memset(colsum[:, :], 0.0)
        return v.memset(stat[:, :], 0.0)
    ve_cs_chain = emit("vector", f_init_v, inc="ve")

    wnat = lambda wd: wd.rearrange("(c p) d -> p c d", p=128)

    def plan_weight_T(wd, dst_slot, scratch_slot, negate=False, sem="sg"):
        sgv = dma("gpsimd", slot[scratch_slot][:, :, :], wnat(wd),
                  waits=[("pe", cnt["pe"]), ("ve", cnt["ve"])], sem=sem)
        vev = 0
        for cd in range(DC):
            tname = f"tp{cd % 2}"
            tpb = tprot[cd % 2]

            def f_tp(t, cd=cd, tpb=tpb):
                last = None
                for cs in range(DC):
                    last = t.transpose(
                        tpb[:, cs * 128:(cs + 1) * 128],
                        slot[scratch_slot][:, cs, cd * 128:(cd + 1) * 128],
                        ident[:, :])
                return last
            pev = emit("tensor", f_tp,
                       waits=[(sem, sgv), ("gp", 1)] + rfree(tname), inc="pe")

            if negate:
                def f_ev(v, cd=cd, tpb=tpb):
                    return v.tensor_scalar(slot[dst_slot][:, cd, :], tpb[:, :],
                                           -1.0, None, ALU.mult)
            else:
                def f_ev(v, cd=cd, tpb=tpb):
                    return v.tensor_copy(slot[dst_slot][:, cd, :], tpb[:, :])
            vev = emit("vector", f_ev, waits=[("pe", pev)], inc="ve")
            evt_free[tname] = ("ve", vev)
        return sgv, vev

    _, ve_wk = plan_weight_T(wk_d, S_KT, S_1N)
    _, ve_wv = plan_weight_T(wv_d, S_VTN, S_1N, negate=True)
    _, ve_w0 = plan_weight_T(w0_d, S_0T, S_1N)
    sg_w1, ve_w1t = plan_weight_T(w1_d, S_1T, S_1N)

    def plan_ln_v2(xbuf, xn_tm, nrb, se1):
        def f_v2(v):
            for rb in range(nrb):
                v.tensor_tensor(stat[:, 28 + rb:29 + rb],
                                stat[:, 24 + rb:25 + rb],
                                stat[:, 24 + rb:25 + rb], ALU.mult)
            for rb in range(nrb):
                v.tensor_tensor(stat[:, 28 + rb:29 + rb],
                                stat[:, 16 + rb:17 + rb],
                                stat[:, 28 + rb:29 + rb], ALU.mult)
            for rb in range(nrb):
                v.tensor_scalar(stat[:, 28 + rb:29 + rb],
                                stat[:, 28 + rb:29 + rb], -0.5, 1.5,
                                ALU.mult, ALU.add)
            for rb in range(nrb):
                v.tensor_tensor(stat[:, 24 + rb:25 + rb],
                                stat[:, 24 + rb:25 + rb],
                                stat[:, 28 + rb:29 + rb], ALU.mult)
            for rb in range(nrb):
                v.tensor_scalar(xn_tm[:, rb, :], xbuf[:, rb, :],
                                stat[:, 12 + rb:13 + rb],
                                stat[:, 24 + rb:25 + rb], ALU.add, ALU.mult)
            return v.memset(stat[:, 0:8], 0.0)
        return emit("vector", f_v2, waits=[("se", se1)], inc="ve")

    def plan_ln_tile(xbuf, xn_tm, nrb, wx, defer_v2=False):
        assert nrb == 4
        def f_v1(v):
            for rb in range(nrb):
                v.reduce_sum(stat[:, rb:rb + 1], xbuf[:, rb, :], axis=AX.X)
            for rb in range(nrb):
                v.scalar_tensor_tensor(xn_tm[:, rb, :], xbuf[:, rb, :], 1.0,
                                       xbuf[:, rb, :], ALU.mult, ALU.mult,
                                       accum_out=stat[:, 4 + rb:5 + rb])
            for rb in range(nrb):
                v.tensor_scalar(stat[:, 8 + rb:9 + rb], stat[:, rb:rb + 1],
                                1.0 / D, None, ALU.mult)
            for rb in range(nrb):
                v.tensor_scalar(stat[:, 12 + rb:13 + rb], stat[:, rb:rb + 1],
                                -1.0 / D, None, ALU.mult)
            for rb in range(nrb):
                v.tensor_scalar(stat[:, 16 + rb:17 + rb],
                                stat[:, 4 + rb:5 + rb], 1.0 / D, 1e-5,
                                ALU.mult, ALU.add)
            for rb in range(nrb):
                v.tensor_tensor(stat[:, 20 + rb:21 + rb],
                                stat[:, 8 + rb:9 + rb],
                                stat[:, 8 + rb:9 + rb], ALU.mult)
            for rb in range(nrb):
                v.tensor_tensor(stat[:, 16 + rb:17 + rb],
                                stat[:, 16 + rb:17 + rb],
                                stat[:, 20 + rb:21 + rb], ALU.subtract)
            last = None
            for rb in range(nrb):
                last = v.reciprocal(stat[:, 24 + rb:25 + rb],
                                    stat[:, 16 + rb:17 + rb])
            return last
        ve1 = emit("vector", f_v1, waits=wx, inc="ve")
        se1 = emit("scalar",
                   lambda s: s.activation(stat[:, 24:28], stat[:, 24:28],
                                          AF.Sqrt),
                   waits=[("ve", ve1)], inc="se")
        if defer_v2:
            return se1

        def f_v2(v):
            for rb in range(nrb):
                v.tensor_tensor(stat[:, 28 + rb:29 + rb],
                                stat[:, 24 + rb:25 + rb],
                                stat[:, 24 + rb:25 + rb], ALU.mult)
            for rb in range(nrb):
                v.tensor_tensor(stat[:, 28 + rb:29 + rb],
                                stat[:, 16 + rb:17 + rb],
                                stat[:, 28 + rb:29 + rb], ALU.mult)
            for rb in range(nrb):
                v.tensor_scalar(stat[:, 28 + rb:29 + rb],
                                stat[:, 28 + rb:29 + rb], -0.5, 1.5,
                                ALU.mult, ALU.add)
            for rb in range(nrb):
                v.tensor_tensor(stat[:, 24 + rb:25 + rb],
                                stat[:, 24 + rb:25 + rb],
                                stat[:, 28 + rb:29 + rb], ALU.mult)
            for rb in range(nrb):
                v.tensor_scalar(xn_tm[:, rb, :], xbuf[:, rb, :],
                                stat[:, 12 + rb:13 + rb],
                                stat[:, 24 + rb:25 + rb], ALU.add, ALU.mult)
            return v.memset(stat[:, 0:8], 0.0)
        return emit("vector", f_v2, waits=[("se", se1)], inc="ve")

    def plan_transpose_fm(src_tm, dst_fm, rb, wsrc, eng="vector"):
        tname = f"tp{rb % 2}"
        tpb = tprot[rb % 2]

        def f_tp(t):
            last = None
            for dc in range(DC):
                last = t.transpose(tpb[:, dc * 128:(dc + 1) * 128],
                                   src_tm[:, dc * 128:(dc + 1) * 128], ident[:, :])
            return last
        pev = emit("tensor", f_tp, waits=list(wsrc) + rfree(tname), inc="pe")
        sem = "ve" if eng == "vector" else "se"

        def f_ev(v):
            if eng == "vector":
                return v.tensor_copy(
                    dst_fm[:, :, rb * 128:(rb + 1) * 128],
                    tpb[:, :].rearrange("p (c r) -> p c r", r=128))
            return v.copy(dst_fm[:, :, rb * 128:(rb + 1) * 128],
                          tpb[:, :].rearrange("p (c r) -> p c r", r=128))
        vev = emit(eng, f_ev, waits=[("pe", pev)], inc=sem)
        evt_free[tname] = (sem, vev)
        return (sem, vev)

    def plan_transpose_tm(src_fm, dst_tm, rb, wsrc):
        tname = f"tp{rb % 2}"
        tpb = tprot[rb % 2]

        def f_tp(t):
            last = None
            for dc in range(DC):
                last = t.transpose(tpb[:, dc * 128:(dc + 1) * 128],
                                   src_fm[:, dc, rb * 128:(rb + 1) * 128],
                                   ident[:, :])
            return last
        pev = emit("tensor", f_tp, waits=list(wsrc) + rfree(tname), inc="pe")
        vev = emit("vector", lambda v: v.tensor_copy(dst_tm[:, rb, :], tpb[:, :]),
                   waits=[("pe", pev)], inc="ve")
        evt_free[tname] = ("ve", vev)
        return vev

    def plan_proj(lhs_slot, lhs_col0, rhs_fm, rcols, reg_i, waits):
        reg = f"mm{reg_i}"

        def f_mm(t):
            last = None
            for ec in range(DC):
                last = t.matmul(mmrot[reg_i][:, :],
                                slot[lhs_slot][:, ec, lhs_col0:lhs_col0 + 128],
                                rhs_fm[:, ec, 0:rcols],
                                start=(ec == 0), stop=(ec == DC - 1))
            return last
        return emit("tensor", f_mm, waits=list(waits) + rfree(reg), inc="pe")

    def load_xA(t, wv):
        sxs = []
        xp = x_d.rearrange("(n p) d -> n p d", p=128)
        for rb in range(RBA):
            sxs.append(dma("sync", xA[:, rb, :], xp[t * RBA + rb], waits=wv))
        return sxs

    g1_first = [True] * DC
    g0_first = [True] * DC
    gdma_hist = []

    sxs0 = load_xA(0, [])
    ln_ev = plan_ln_tile(xA, xn_tmA, RBA, [("sx", sxs0[-1])])

    def emit_fm_block(t, ln_e, cs_guard):
        w = [("ve", ln_e)] + ([("ve", cs_guard)] if cs_guard else [])
        fm_ev = [plan_transpose_fm(xn_tmA[:, rb, :], xn_fmA, rb, w,
                                   eng="scalar") for rb in range(RBA)]
        return [fm_ev[-1]], cnt["pe"]

    def emit_k_block(xn_ready):
        k_ev = []
        for dc in range(DC):
            pev = plan_proj(S_KT, dc * 128, xn_fmA, TA, dc % 2,
                            xn_ready + [("ve", ve_wk)])
            vev = emit("vector",
                       lambda v, dc=dc: v.tensor_copy(k_fm[:, dc, :],
                                                      mmrot[dc % 2][:, :]),
                       waits=[("pe", pev)], inc="ve")
            evt_free[f"mm{dc % 2}"] = ("ve", vev)
            k_ev.append(vev)
        return [("ve", max(k_ev))]

    xn_ready_c, tp_pe_c = emit_fm_block(0, ln_ev, None)
    k_ready_c = emit_k_block(xn_ready_c)

    for t in range(NTA):
        tile_barrier()
        ve_ln = ln_ev
        ve_xn = [ve_ln] * RBA
        xn_ready = xn_ready_c
        k_ready = k_ready_c
        if debug and t == 0:
            dma("gpsimd", dbg["dbg_xa"][:, :, :], xA[:, :, :],
                waits=[("ve", ve_xn[-1])])
            dma("gpsimd", dbg["dbg_stat"][:, :], stat[:, :],
                waits=[("ve", ve_xn[-1])])
            dma("gpsimd", dbg["dbg_xntm"][:, :, :], xn_tmA[:, :, :],
                waits=[("ve", ve_xn[-1])])

        ktm_ev = [plan_transpose_tm(k_fm, k_tm, rb, k_ready) for rb in range(RBA)]

        p1_ev = []
        for dc in range(DC):
            pev = plan_proj(S_0T, dc * 128, k_fm, TA, dc % 2,
                            k_ready + [("ve", ve_w0)])
            vev = emit("vector",
                       lambda v, dc=dc: v.tensor_copy(pre1_sb[:, dc, :],
                                                      mmrot[dc % 2][:, :]),
                       waits=[("pe", pev)], inc="ve")
            evt_free[f"mm{dc % 2}"] = ("ve", vev)
            p1_ev.append(vev)
        p1_ready = [("ve", max(p1_ev))]

        se_a1 = 0
        for dc in range(DC):
            se_a1 = emit("scalar",
                         lambda s, dc=dc: s.activation(
                             a1_fm[:, dc, :], pre1_sb[:, dc, :], AF.Silu),
                         waits=[("ve", p1_ev[dc])], inc="se")
        def f_dsilu(s):
            last = None
            for dc in range(DC):
                last = s.activation(deriv[:, dc, :], pre1_sb[:, dc, :],
                                    AF.Derivative_silu)
            return last
        se_dv = emit("scalar", f_dsilu, waits=[("se", se_a1)], inc="se")
        a1_ready = [("se", se_a1)]
        a1tm_ev = [plan_transpose_tm(a1_fm, a1_tm, rb, a1_ready)
                   for rb in range(RBA)]

        d2_ev = []
        for dc in range(DC):
            reg = f"mm{dc % 2}"

            def f_mm(t_, dc=dc):
                last = None
                for ec in range(DC):
                    t_.matmul(mmrot[dc % 2][:, :],
                              slot[S_1T][:, ec, dc * 128:(dc + 1) * 128],
                              a1_fm[:, ec, :], start=(ec == 0), stop=False)
                for ec in range(DC):
                    last = t_.matmul(mmrot[dc % 2][:, :],
                                     slot[S_VTN][:, ec, dc * 128:(dc + 1) * 128],
                                     xn_fmA[:, ec, :], start=False,
                                     stop=(ec == DC - 1))
                return last
            pev = emit("tensor", f_mm,
                       waits=a1_ready + xn_ready
                       + [("ve", ve_wv), ("ve", ve_w1t)] + rfree(reg), inc="pe")
            vev = emit("vector",
                       lambda v, dc=dc: v.tensor_scalar(
                           d2_fm[:, dc, :], mmrot[dc % 2][:, :],
                           5.0, -5.0, ALU.min, ALU.max),
                       waits=[("pe", pev)], inc="ve")
            evt_free[reg] = ("ve", vev)
            d2_ev.append(vev)
        d2_ready = [("ve", max(d2_ev))]

        def f_cs(v):
            for dc in range(DC):
                v.reduce_sum(cs_tmp[:, dc:dc + 1], xn_fmA[:, dc, :], axis=AX.X)
            v.tensor_tensor(colsum[:, 0:4], colsum[:, 0:4], cs_tmp[:, 0:4],
                            ALU.add)
            return v.tensor_tensor(colsum[:, 4:8], colsum[:, 4:8],
                                   cs_tmp[:, 4:8], ALU.add)
        ve_cs_chain = emit("vector", f_cs,
                           waits=xn_ready + [("ve", ve_cs_chain)], inc="ve")

        d2tm_ev = [plan_transpose_tm(d2_fm, d2_tm, rb, d2_ready)
                   for rb in range(RBA)]

        if t + 1 < NTA:
            sxs = load_xA(t + 1, [("ve", ve_ln)])
            ln_ev = plan_ln_tile(xA, xn_tmA, RBA,
                                 [("sx", sxs[-1]), ("pe", tp_pe_c)])
            xn_ready_c, tp_pe_c = emit_fm_block(t + 1, ln_ev, ve_cs_chain)
            k_ready_c = emit_k_block(xn_ready_c)

        d1_ev = []
        for dc in range(DC):
            pev = plan_proj(S_1N, dc * 128, d2_fm, TA, dc % 2,
                            d2_ready + [("sg", sg_w1), ("se", se_dv)])
            vev = emit("vector",
                       lambda v, dc=dc: v.scalar_tensor_tensor(
                           d1_fm[:, dc, :], mmrot[dc % 2][:, :], CSCALE,
                           deriv[:, dc, :], ALU.mult, ALU.mult),
                       waits=[("pe", pev)], inc="ve")
            evt_free[f"mm{dc % 2}"] = ("ve", vev)
            d1_ev.append(vev)
        d1_ready = [("ve", max(d1_ev))]
        d1tm_ev = [plan_transpose_tm(d1_fm, d1_tm, rb, d1_ready)
                   for rb in range(RBA)]

        a1tm_r = [("ve", max(a1tm_ev))]
        d2tm_r = [("ve", max(d2tm_ev))]
        ktm_r = [("ve", max(ktm_ev))]
        d1tm_r = [("ve", max(d1tm_ev))]

        def plan_g(ec, lhs_tm, rhs_tm, dram_fn, first_flags, wop):
            gname = "gA" if ec % 2 == 0 else "gB"
            gps = gA if ec % 2 == 0 else gB

            def f_mm(t_):
                last = None
                for hf in range(2):
                    for rb in range(RBA):
                        last = t_.matmul(
                            gps[:, hf * 512:(hf + 1) * 512],
                            lhs_tm[:, rb, ec * 128:(ec + 1) * 128],
                            rhs_tm[:, rb, hf * 512:(hf + 1) * 512],
                            start=(rb == 0), stop=(rb == RBA - 1))
                return last
            pev = emit("tensor", f_mm, waits=wop + rfree(gname), inc="pe")
            gst = gstage[plan_g.gidx % 4]
            plan_g.gidx += 1
            gw = [("pe", pev), ("se", se_dv)]
            if len(gdma_hist) >= 4:
                gw.append(("sg", gdma_hist[-4]))
            sev = emit("scalar", lambda s: s.copy(gst[:, :], gps[:, :]),
                       waits=gw, inc="se")
            evt_free[gname] = ("se", sev)
            kw = {} if first_flags[ec] else {"accum_op": ALU.add}
            first_flags[ec] = False
            sgv = dma("gpsimd", dram_fn(gin_d, ec), gst[:, :],
                      waits=[("se", sev)], **kw)
            gdma_hist.append(sgv)

        if not hasattr(plan_g, "gidx"):
            plan_g.gidx = 0
        for ec in range(DC):
            plan_g(ec, a1_tm, d2_tm, g1t_dr, g1_first, a1tm_r + d2tm_r)
        for ec in range(DC):
            plan_g(ec, k_tm, d1_tm, g0t_dr, g0_first, ktm_r + d1tm_r)

        if debug and t == 0:
            for nm, buf in [("dbg_xn", xn_fmA), ("dbg_k", k_fm),
                            ("dbg_a1", a1_fm), ("dbg_d2", d2_fm),
                            ("dbg_d1", d1_fm)]:
                dma("gpsimd", dbg[nm][:, :, :], buf[:, :, :],
                    waits=[("se", cnt["se"]), ("ve", cnt["ve"]),
                           ("pe", cnt["pe"])])
            dma("gpsimd", dbg["dbg_ktm"][:, :, :], k_tm[:, :, :])

    dma("gpsimd", colsum_dr(gin_d), colsum[:, :],
        waits=[("ve", ve_cs_chain)])
    sg_all_g = cnt["sg"]
    pe_endA = cnt["pe"]

    tile_barrier()
    ve_wq = ve_wo = 0
    S_QT, S_OT = S_KT, S_VTN

    def plan_bcast_f(srcap, dstcol, mult, waits=()):
        pev = emit("tensor",
                   lambda t: t.matmul(mm0[:, 0:1], ones_r_f[:, :], srcap,
                                      start=True, stop=True),
                   waits=[("gp", 1)] + list(waits) + rfree("mm0"), inc="pe")
        vev = emit("vector",
                   lambda v: v.tensor_scalar(scal[:, dstcol:dstcol + 1],
                                             mm0[:, 0:1], mult, None, ALU.mult),
                   waits=[("pe", pev)], inc="ve")
        evt_free["mm0"] = ("ve", vev)
        return vev

    ve_s0 = plan_bcast_f(sc11[:, 0:1], C_S0X2, 2.0, waits=[("sx", sx_sc[0])])
    ve_s1 = plan_bcast_f(sc11[:, 1:2], C_S1X2, 2.0, waits=[("sx", sx_sc[1])])
    ve_s2a = plan_bcast_f(sc11[:, 2:3], C_S2X2, 2.0, waits=[("sx", sx_sc[2])])
    ve_bgd = plan_bcast_f(sc11[:, 3:4], C_BGD, 1.0, waits=[("sx", sx_sc[3])])
    ve_bgl = plan_bcast_f(sc11[:, 4:5], C_BGL, 1.0, waits=[("sx", sx_sc[4])])

    S_0NT, S_1NT = S_0T, S_1T
    hold = {}

    def plan_gates_and_wn():
        sx_csr = dma("sync", cs_tmp[:, :], colsum_dr(gout_d), waits=[("cc", 1)])

        def f_gdot(v):
            v.tensor_tensor(gate_sc[:, 8:8 + DC], cs_tmp[:, :], wg_sb[:, 0:DC],
                            ALU.mult)
            v.tensor_tensor(gate_sc[:, 16:16 + DC], cs_tmp[:, :],
                            wg_sb[:, DC:2 * DC], ALU.mult)
            v.reduce_sum(gate_sc[:, 0:1], gate_sc[:, 8:8 + DC], axis=AX.X)
            return v.reduce_sum(gate_sc[:, 1:2], gate_sc[:, 16:16 + DC],
                                axis=AX.X)
        ve_gd = emit("vector", f_gdot,
                     waits=[("sx", sx_csr), ("sx", sx_scal), ("pe", pe_endA)],
                     inc="ve")

        def plan_gate(src_col, dst_col, bias_col):
            pev = emit("tensor",
                       lambda t: t.matmul(mm1[0:1, 0:1], ones_c_f[:, :],
                                          stat[:, src_col:src_col + 1],
                                          start=True, stop=True),
                       waits=[("ve", ve_gd), ("gp", 1)] + rfree("mm1"), inc="pe")
            vev = emit("vector",
                       lambda v: v.tensor_copy(gate_sc[0:1, dst_col:dst_col + 1],
                                               mm1[0:1, 0:1]),
                       waits=[("pe", pev)], inc="ve")
            evt_free["mm1"] = ("ve", vev)
            pev2 = emit("tensor",
                        lambda t: t.matmul(mm1[:, 0:1], ones_r_f[:, :],
                                           gate_sc[0:1, dst_col:dst_col + 1],
                                           start=True, stop=True),
                        waits=[("ve", vev)] + rfree("mm1"), inc="pe")
            vev2 = emit("vector",
                        lambda v: v.scalar_tensor_tensor(
                            gate_sc[:, dst_col:dst_col + 1], mm1[:, 0:1],
                            1.0 / 32768.0,
                            scal[:, (C_BGD if bias_col == 0 else C_BGL):
                                 (C_BGD if bias_col == 0 else C_BGL) + 1],
                            ALU.mult, ALU.add),
                        waits=[("pe", pev2), ("ve", ve_bgd), ("ve", ve_bgl)],
                        inc="ve")
            evt_free["mm1"] = ("ve", vev2)
            return vev2

        ve_zd = plan_gate(0, 2, 0)
        ve_zl = plan_gate(1, 3, 1)

        def f_sig(s):
            s.activation(gate_sc[:, 2:3], gate_sc[:, 2:3], AF.Sigmoid)
            return s.activation(gate_sc[:, 3:4], gate_sc[:, 3:4], AF.Sigmoid)
        se_sig = emit("scalar", f_sig, waits=[("ve", ve_zd), ("ve", ve_zl)], inc="se")

        def f_gder(v):
            v.tensor_scalar(scal[:, C_1MA:C_1MA + 1], gate_sc[:, 2:3], -1.0, 1.0,
                            ALU.mult, ALU.add)
            v.tensor_scalar(scal[:, C_NLR:C_NLR + 1], gate_sc[:, 3:4], -1.0, None,
                            ALU.mult)
            return v.tensor_scalar(scal[:, C_NLRC:C_NLRC + 1], gate_sc[:, 3:4],
                                   -CSCALE, None, ALU.mult)
        ve_gder = emit("vector", f_gder, waits=[("se", se_sig)], inc="ve")

        if debug:
            dma("gpsimd", dbg["dbg_cs"][:, :], cs_tmp[:, :], waits=[("sx", sx_csr)])
            dma("gpsimd", dbg["dbg_gate"][:, 0:8], gate_sc[:, 0:8],
                waits=[("ve", ve_gder)])

        wn_last = ve_gder
        first_form = True
        gld = [gtmp, outst[1]]
        form_prev = [wn_last, wn_last]
        it = 0
        for slot_i, dram_fn, clip_v, nlr_col in [
                (S_0T, g0t_dr, 1.0, C_NLR), (S_1T, g1t_dr, CLIP_HI, C_NLRC)]:
            for ec in range(DC):
                for hf in range(2):
                    gb = gld[it % 2]
                    src = dram_fn(gout_d, ec)[:, hf * 512:(hf + 1) * 512]
                    sxv = dma("sync", gb[:, :], src,
                              waits=[("cc", 1), ("ve", form_prev[it % 2])])

                    def f_form(v, slot_i=slot_i, ec=ec, hf=hf, clip_v=clip_v,
                               nlr_col=nlr_col, gb=gb):
                        sl = slot[slot_i][:, ec, hf * 512:(hf + 1) * 512]
                        v.tensor_scalar(outst[0][:, :], gb[:, :], clip_v,
                                        -clip_v, ALU.min, ALU.max)
                        v.tensor_scalar(gb[:, :], sl,
                                        scal[:, C_1MA:C_1MA + 1], None,
                                        ALU.mult)
                        return v.scalar_tensor_tensor(
                            sl, outst[0][:, :], scal[:, nlr_col:nlr_col + 1],
                            gb[:, :], ALU.mult, ALU.add)
                    fw = [("sx", sxv)]
                    if first_form:
                        fw.append(("pe", pe_endA))
                        first_form = False
                    wn_last = emit("vector", f_form, waits=fw, inc="ve")
                    form_prev[it % 2] = wn_last
                    it += 1
        ve_wn = wn_last

        if debug:
            dma("gpsimd", dbg["dbg_w0n"][:, :, :], slot[S_0NT][:, :, :],
                waits=[("ve", ve_wn)])
            dma("gpsimd", dbg["dbg_w1n"][:, :, :], slot[S_1NT][:, :, :],
                waits=[("ve", ve_wn)])
        hold["ve_wn"] = ve_wn


    xp = x_d.rearrange("(n p) d -> n p d", p=128)
    k0p = k0_d.rearrange("(c p) j -> p c j", p=128)
    k1p = k1_d.rearrange("(c p) j -> p c j", p=128)
    k2p = k2_d.rearrange("(c p) d -> p c d", p=128)
    k2bfp2 = k2bf_d.rearrange("(c p) d -> p c d", p=128)
    outp = out_d.rearrange("(n p) d -> p n d", p=128) if False else \
        out_d.rearrange("(n p) d -> n p d", p=128)

    f3r = [("mm0", mm0[:, :]), ("mm1", mm1[:, :]),
           ("gA0", gA[:, 0:512]), ("gA1", gA[:, 512:1024])]
    f2r = f3r + [("gB0", gB[:, 0:512]), ("gB1", gB[:, 512:1024])]
    if "gA" in evt_free:
        evt_free["gA0"] = evt_free["gA1"] = evt_free["gA"]
    if "gB" in evt_free:
        evt_free["gB0"] = evt_free["gB1"] = evt_free["gB"]
    k0_reader = [0, 0, 0]
    k1_reader = [0, 0, 0, 0]
    w2_reader = [pe_endA, pe_endA, pe_endA]
    out_dma = [0, 0]

    def tile_part1(t, ln_ev, sync_k1=False, sync_k0=False, mid_cb=None):
        fm_ev = [plan_transpose_fm(xn_tmC[:, rb, :], xn_fmC, rb,
                                   [("ve", ln_ev)], eng="scalar")
                 for rb in range(RBC)]
        xn_ready = [fm_ev[-1]]
        tp_pe = cnt["pe"]

        for jg in range(8):
            bi = jg % 2
            if sync_k0:
                cvl = 0
                for qtr in range(4):
                    qp = tile_part1.k0qi % 2
                    prev = tile_part1.k0u8_rd[qp]
                    ldw = [("ve", prev)] if prev else []
                    bu = k0u8s[qp]
                    sxb = dma("sync", bu[:, :, :],
                              k0p[:, qtr * 4:(qtr + 1) * 4,
                                  jg * 512:(jg + 1) * 512],
                              waits=ldw, sem="k0" + "ab"[qp])
                    cw = [("k0" + "ab"[qp], sxb)]
                    if qtr == 0:
                        cw += [("pe", k0_reader[bi])] if k0_reader[bi] \
                            else [("pe", pe_endA)]
                    cvl = emit("vector",
                               lambda v, bi=bi, qtr=qtr, bu=bu: v.tensor_copy(
                                   k0s[bi][:, qtr * 4:(qtr + 1) * 4, :],
                                   bu[:, :, :]),
                               waits=cw, inc="ve")
                    tile_part1.k0u8_rd[qp] = cvl
                    tile_part1.k0qi += 1
                ve_w = emit("vector",
                            lambda v, bi=bi: v.tensor_tensor(
                                k0s[bi][:, 0:DC, :], k0s[bi][:, 0:DC, :],
                                k0s[bi][:, DC:2 * DC, :], ALU.subtract),
                            waits=[("ve", cvl)], inc="ve")
            else:
                k0w = [("pe", k0_reader[bi])] if k0_reader[bi] else \
                    [("ve", ve_wo), ("pe", pe_endA)]
                sgb = dma("gpsimd", k0s[bi][:, :, :],
                          k0p[:, :, jg * 512:(jg + 1) * 512], waits=k0w,
                          sem=f"ka{bi}")
                ve_w = emit("vector",
                            lambda v, bi=bi: v.tensor_tensor(
                                k0s[bi][:, 0:DC, :], k0s[bi][:, 0:DC, :],
                                k0s[bi][:, DC:2 * DC, :], ALU.subtract),
                            waits=[(f"ka{bi}", sgb)], inc="ve")
            for jj in range(4):
                jc = jg * 4 + jj
                reg = f"mm{jj % 2}"

                def f_mm(t_, jj=jj, bi=bi):
                    last = None
                    for ec in range(DC):
                        last = t_.matmul(mmrot[jj % 2][:, :],
                                         k0s[bi][:, ec, jj * 128:(jj + 1) * 128],
                                         xn_fmC[:, ec, :],
                                         start=(ec == 0), stop=(ec == DC - 1))
                    return last
                pev = emit("tensor", f_mm,
                           waits=xn_ready + [("ve", ve_w)] + rfree(reg),
                           inc="pe")
                sev = emit("scalar",
                           lambda s, jc=jc, jj=jj: s.activation(
                               u1[:, jc, :], mmrot[jj % 2][:, :], AF.Relu,
                               scale=scal[:, C_S0X2:C_S0X2 + 1]),
                           waits=[("pe", pev), ("ve", ve_s0)], inc="se")
                evt_free[reg] = ("se", sev)
            k0_reader[bi] = cnt["pe"]
        u1_ready = [("se", cnt["se"])]

        if mid_cb is not None:
            mid_cb()

        q_ev = []
        for dc in range(DC):
            pev = plan_proj(S_QT, dc * 128, xn_fmC, TC, dc % 2,
                            xn_ready + [("ve", ve_wq)])
            vev = emit("vector",
                       lambda v, dc=dc: v.tensor_copy(q_fm[:, dc, :],
                                                      mmrot[dc % 2][:, :]),
                       waits=[("pe", pev)], inc="ve")
            evt_free[f"mm{dc % 2}"] = ("ve", vev)
            q_ev.append(vev)
        q_ready = [("ve", max(q_ev))]

        for jg in range(8):
            jcs = [jg * 4 + jj for jj in range(4)]
            regs = [f2r[jc % 6] for jc in jcs]
            pev = 0
            for eo8 in range(8):
                bi = eo8 % 4
                k1w = [("pe", k1_reader[bi])] if k1_reader[bi] else \
                    ([("ve", ve_wo), ("pe", pe_endA)] + u1_ready)
                if sync_k1:
                    gi = jg * 8 + eo8
                    bu = [k1u8, k1u8b][gi % 2]
                    prev_cv = tile_part1.k1u8_rd[gi % 2]
                    ldw = [("ve", prev_cv)] if prev_cv else list(k1w)
                    sxv = dma("sync", bu[:, :, :],
                              k1p[:, eo8 * 4:(eo8 + 1) * 4,
                                  jg * 512:(jg + 1) * 512], waits=ldw,
                              sem="ku")
                    ve_cv = emit("vector",
                                 lambda v, bi=bi, bu=bu: v.tensor_scalar(
                                     k1_s[bi][:, :, :], bu[:, :, :],
                                     2.0, -1.0, ALU.mult, ALU.add),
                                 waits=[("ku", sxv)] + k1w, inc="ve")
                    tile_part1.k1u8_rd[gi % 2] = ve_cv
                else:
                    sgv = dma("gpsimd", k1_s[bi][:, :, :],
                              k1p[:, eo8 * 4:(eo8 + 1) * 4,
                                  jg * 512:(jg + 1) * 512], waits=k1w,
                              sem=f"kb{bi}")
                    ve_cv = emit("vector",
                                 lambda v, bi=bi: v.tensor_scalar(
                                     k1_s[bi][:, :, :], k1_s[bi][:, :, :],
                                     2.0, -1.0, ALU.mult, ALU.add),
                                 waits=[(f"kb{bi}", sgv)], inc="ve")

                def f_mm(t_, eo8=eo8, bi=bi, regs=regs):
                    last = None
                    for jj in range(4):
                        for ecq in range(4):
                            last = t_.matmul(
                                regs[jj][1],
                                k1_s[bi][:, ecq, jj * 128:(jj + 1) * 128],
                                u1[:, eo8 * 4 + ecq, :],
                                start=(eo8 == 0 and ecq == 0),
                                stop=(eo8 == 7 and ecq == 3))
                    return last
                mw = [("ve", ve_cv)] + u1_ready
                if eo8 == 0:
                    for nm, _ in regs:
                        mw += rfree(nm)
                pev = emit("tensor", f_mm, waits=mw, inc="pe")
                k1_reader[bi] = cnt["pe"]
            for jj, jc in enumerate(jcs):
                sev = emit("scalar",
                           lambda s, jc=jc, jj=jj, regs=regs: s.activation(
                               u2[:, jc, :], regs[jj][1], AF.Relu,
                               scale=scal[:, C_S1X2:C_S1X2 + 1]),
                           waits=[("pe", pev), ("ve", ve_s1)], inc="se")
                evt_free[regs[jj][0]] = ("se", sev)
        ve_u2s = emit("vector",
                      lambda v: v.tensor_scalar(u2[:, :, :], u2[:, :, :],
                                                scal[:, C_S2X2:C_S2X2 + 1],
                                                None, ALU.mult),
                      waits=[("se", cnt["se"]), ("ve", ve_s2a)], inc="ve")
        u2_ready = [("se", cnt["se"]), ("ve", ve_u2s)]
        return dict(q_ready=q_ready, u1_ready=u1_ready,
                    u2_ready=u2_ready, tp_pe=tp_pe, ln_ev=ln_ev)

    def tile_part2(t, st):
        q_ready, u2_ready = st["q_ready"], st["u2_ready"]
        s_ev = []
        for dc in range(DC):
            pev = plan_proj(S_0NT, dc * 128, q_fm, TC, dc % 2,
                            q_ready + [("ve", hold["ve_wn"])])
            sg_guard = hold["k2bf_ready"] if (t == 0 and dc == 0) else []
            sev = emit("scalar",
                       lambda s, dc=dc: s.activation(s_fm[:, dc, :],
                                                     mmrot[dc % 2][:, :],
                                                     AF.Silu),
                       waits=[("pe", pev)] + sg_guard, inc="se")
            evt_free[f"mm{dc % 2}"] = ("se", sev)
            s_ev.append(sev)
        s_ready = [("se", max(s_ev))]
        h_ev = []
        for dc in range(DC):
            pev = plan_proj(S_1NT, dc * 128, s_fm, TC, dc % 2, s_ready)
            vev = emit("vector",
                       lambda v, dc=dc: v.tensor_copy(h_fm[:, dc, :],
                                                      mmrot[dc % 2][:, :]),
                       waits=[("pe", pev)] + q_ready, inc="ve")
            evt_free[f"mm{dc % 2}"] = ("ve", vev)
            h_ev.append(vev)
        h_ready = [("ve", max(h_ev))]

        ln_se = None
        if t + 1 < NTC:
            sxs = []
            for rb in range(RBC):
                sxs.append(dma("sync", xC[:, rb, :], xp[(t + 1) * RBC + rb],
                               waits=[("ve", st["ln_ev"])]))
            ln_se = plan_ln_tile(xC, xn_tmC, RBC,
                                 [("sx", sxs[-1]), ("pe", st["tp_pe"])],
                                 defer_v2=True)

        if debug and t == 0:
            dma("gpsimd", dbg["dbg_q"][:, :, :], q_fm[:, :, :],
                waits=q_ready + [("pe", cnt["pe"])])
            dma("gpsimd", dbg["dbg_u1"][:, :, :], u1[:, :, :],
                waits=st["u1_ready"])
            dma("gpsimd", dbg["dbg_u2"][:, :, :], u2[:, :, :], waits=u2_ready)
            dma("gpsimd", dbg["dbg_h"][:, :, :], h_fm[:, :, :], waits=h_ready)

        ln_v2 = [None]
        for hf in range(2):
            if hf == 1 and ln_se is not None:
                ln_v2[0] = plan_ln_v2(xC, xn_tmC, RBC, ln_se)
            pev = 0
            for grp in range(16):
                bi = grp % 3
                w2w = [("pe", w2_reader[bi])]
                if t == 0 and hf == 0 and grp == 0:
                    w2w += hold["k2bf_ready"] + hold["k1u8b_free"]
                sgv = dma("gpsimd", w2s[bi][:, :, :],
                          k2bfp2[:, grp * 2:(grp + 1) * 2,
                                 hf * 512:(hf + 1) * 512], waits=w2w,
                          sem=f"kc{bi}")

                def f_mm(t_, grp=grp, bi=bi):
                    last = None
                    for rb in range(RBC):
                        for jj in range(2):
                            jc = grp * 2 + jj
                            last = t_.matmul(
                                f3r[rb][1],
                                u2[:, jc, rb * 128:(rb + 1) * 128],
                                w2s[bi][:, jj, :],
                                start=(jc == 0), stop=False)
                    return last
                mw = [(f"kc{bi}", sgv)] + u2_ready
                if grp == 0:
                    for nm, _ in f3r:
                        mw += rfree(nm)
                pev = emit("tensor", f_mm, waits=mw, inc="pe")
                w2_reader[bi] = cnt["pe"]

            def f_ao(t_, hf=hf):
                last = None
                for rb in range(RBC):
                    for ec in range(DC):
                        last = t_.matmul(
                            f3r[rb][1],
                            h_fm[:, ec, rb * 128:(rb + 1) * 128],
                            slot[S_OT][:, ec, hf * 512:(hf + 1) * 512],
                            start=False, stop=(ec == DC - 1))
                return last
            pev = emit("tensor", f_ao, waits=h_ready + [("ve", ve_wo)],
                       inc="pe")

            for rb in range(RBC):
                ob = outst[rb % 2]
                lw = []
                if out_dma[rb % 2]:
                    lw.append(("sx", out_dma[rb % 2]))
                sxr = dma("sync", ob[:, :],
                          xp[t * RBC + rb][:, hf * 512:(hf + 1) * 512],
                          waits=lw)
                vev = emit("vector",
                           lambda v, rb=rb, ob=ob: v.tensor_tensor(
                               ob[:, :], f3r[rb][1], ob[:, :], ALU.add),
                           waits=[("pe", pev), ("sx", sxr)], inc="ve")
                evt_free[f3r[rb][0]] = ("ve", vev)
                out_dma[rb % 2] = dma(
                    "sync", outp[t * RBC + rb][:, hf * 512:(hf + 1) * 512],
                    ob[:, :], waits=[("ve", vev)])
        if ln_se is None:
            return None
        return ln_v2[0]

    sxs0c = []
    for rb in range(RBC):
        sxs0c.append(dma("sync", xC[:, rb, :], xp[rb],
                         waits=[("ve", ln_ev)]))
    ln0c = plan_ln_tile(xC, xn_tmC, RBC, [("sx", sxs0c[-1]), ("pe", pe_endA)])

    sw_wq = dma("gpsimd", slot[S_1N][:, :, :], wnat(wq_d),
                waits=[("pe", pe_endA)], sem="sw")
    sw_wo = dma("gpsimd", wout_nat[:, :, :], wnat(wo_d),
                waits=[("pe", pe_endA)], sem="sw")
    emit("gpsimd",
         lambda g: g.collective_compute(
             "AllReduce", ALU.add, replica_groups=[list(range(NCORES))],
             ins=[gin_d.opt()], outs=[gout_d.opt()]),
         waits=[("sg", sg_all_g)], inc="cc")

    def emit_weight_T():
        nonlocal ve_wq, ve_wo
        for src_ap, dst, swv in ((slot[S_1N], S_QT, sw_wq),
                                 (wout_nat, S_OT, sw_wo)):
            vev = 0
            for cd in range(DC):
                tname = f"tp{cd % 2}"
                tpb = tprot[cd % 2]

                def f_tp(t, cd=cd, tpb=tpb, src_ap=src_ap):
                    last = None
                    for cs in range(DC):
                        last = t.transpose(
                            tpb[:, cs * 128:(cs + 1) * 128],
                            src_ap[:, cs, cd * 128:(cd + 1) * 128],
                            ident[:, :])
                    return last
                pev = emit("tensor", f_tp,
                           waits=[("sw", swv), ("gp", 1)] + rfree(tname),
                           inc="pe")
                vev = emit("vector",
                           lambda v, cd=cd, tpb=tpb, dst=dst: v.tensor_copy(
                               slot[dst][:, cd, :], tpb[:, :]),
                           waits=[("pe", pev)], inc="ve")
                evt_free[tname] = ("ve", vev)
            if dst == S_QT:
                ve_wq = vev
            else:
                ve_wo = vev
        hold["woT_pe"] = cnt["pe"]

    tile_part1.k1u8_rd = [0, 0]
    tile_part1.k0u8_rd = [0, 0]
    tile_part1.k0qi = 0
    st0 = tile_part1(0, ln0c, sync_k1=True, sync_k0=True,
                     mid_cb=emit_weight_T)
    plan_gates_and_wn()

    k2bfp = k2bf_d.rearrange("(c p) j -> p c j", p=128)
    k2_st = [0, 0]
    for g16 in range(16):
        p = g16 % 2
        ldw = [(f"ps{p}", k2_st[p])] if k2_st[p] else \
            [("pe", hold["woT_pe"])]
        ldv = dma("gpsimd", k2st[p][:, :, :], k2p[:, 2 * g16:2 * g16 + 2, :],
                  waits=ldw, sem=f"pl{p}")
        cvv = emit("vector",
                   lambda v, p=p: v.tensor_scalar(
                       k2st[p][:, :, :], k2st[p][:, :, :],
                       2.0, -1.0, ALU.mult, ALU.add),
                   waits=[(f"pl{p}", ldv)], inc="ve")
        k2_st[p] = dma("sync", k2bfp[:, 2 * g16:2 * g16 + 2, :],
                       k2st[p][:, :, :], waits=[("ve", cvv)], sem=f"ps{p}")
    hold["k2bf_ready"] = [("ps0", k2_st[0]), ("ps1", k2_st[1])]
    hold["k1u8b_free"] = [("ve", max(tile_part1.k1u8_rd))]
    if debug:
        dma("gpsimd", dbg["dbg_g1"][:, :],
            gout_d[0:DC * 128 * D].rearrange("(p d) -> p d", p=DC * 128),
            waits=[("cc", 1)])
        dma("gpsimd", dbg["dbg_g0"][:, :],
            gout_d[DC * 128 * D:2 * DC * 128 * D].rearrange(
                "(p d) -> p d", p=DC * 128),
            waits=[("cc", 1)])
    ln_ev_c = tile_part2(0, st0)
    for t in range(1, NTC):
        tile_barrier()
        st = tile_part1(t, ln_ev_c)
        ln_ev_c = tile_part2(t, st)

    sx_fin, sg_fin = cnt["sx"], cnt["sg"]
    emit("sync", lambda e: e.nop(), waits=[("sx", sx_fin)])
    fin_waits = [("sg", sg_fin), ("sx", sx_fin)] +         [(s, cnt[s]) for s in ["ka0", "ka1", "kb0", "kb1", "kc0", "kc1"]]
    emit("gpsimd", lambda e: e.nop(), waits=fin_waits)

    with nc.Block() as block:
        def runner(eng_name):
            def body(e):
                for w, fn, inc, amt in plan[eng_name]:
                    for (s, v) in w:
                        e.wait_ge(sems[s], v)
                    inst = fn(e)
                    if inc is not None and inst is not None:
                        inst.then_inc(sems[inc], amt)
            return body
        block.tensor(runner("tensor"))
        block.vector(runner("vector"))
        block.scalar(runner("scalar"))
        block.gpsimd(runner("gpsimd"))
        block.sync(runner("sync"))

    return nc


_NC_CACHE = {}


def kernel(**inputs):
    x = np.ascontiguousarray(np.asarray(inputs["L0_attn_x"], dtype=np.float32))
    f32 = lambda k: np.ascontiguousarray(np.asarray(inputs[k], dtype=np.float32))
    b2u = lambda k: np.ascontiguousarray(
        np.asarray(inputs[k]).astype(np.uint8, copy=False))
    common = {
        "Wk": f32("L0_attn_Wk"), "Wv": f32("L0_attn_Wv"),
        "Wq": f32("L0_attn_Wq"), "Wout": f32("L0_attn_Wout"),
        "W0": f32("L0_attn_W0"), "W1": f32("L0_attn_W1"),
        "Wg_decay": f32("L0_attn_Wg_decay").reshape(1, D),
        "bg_decay": f32("L0_attn_bg_decay").reshape(1, 1),
        "Wg_lr": f32("L0_attn_Wg_lr").reshape(1, D),
        "bg_lr": f32("L0_attn_bg_lr").reshape(1, 1),
        "k0": b2u("L0_ffn_k0"), "k1": b2u("L0_ffn_k1"), "k2": b2u("L0_ffn_k2"),
        "s0": f32("L0_ffn_s0").reshape(1, 1),
        "s1": f32("L0_ffn_s1").reshape(1, 1),
        "s2": f32("L0_ffn_s2").reshape(1, 1),
    }
    if "nc" not in _NC_CACHE:
        _NC_CACHE["nc"] = build_nc()
    nc = _NC_CACHE["nc"]
    in_maps = [dict(common, x=np.ascontiguousarray(x[i])) for i in range(NCORES)]
    res = run_bass_kernel_spmd(nc, in_maps, core_ids=list(range(NCORES)))
    return np.stack([np.asarray(res.results[i]["out"]) for i in range(NCORES)], 0)


# version 22
# speedup vs baseline: 1.0115x; 1.0115x over previous
import numpy as np
import concourse.bass as bass
import concourse.mybir as mybir
from concourse.bass_utils import run_bass_kernel_spmd

F32 = mybir.dt.float32
BF16 = mybir.dt.bfloat16
U8 = mybir.dt.uint8
AF = mybir.ActivationFunctionType
ALU = mybir.AluOpType
AX = mybir.AxisListType

NCORES = 8
R = 4096
D = 1024
DC = D // 128
H = 4096
HC = H // 128
TA = 512
NTA = R // TA
RBA = TA // 128
TC = 512
NTC = R // TC
RBC = TC // 128
CSCALE = 2.0 / 4096.0
CLIP_HI = 2048.0


def build_nc(debug=False):
    nc = bass.Bass("TRN2", target_bir_lowering=False)

    x_d = nc.dram_tensor("x", [R, D], F32, kind="ExternalInput").ap()
    wk_d = nc.dram_tensor("Wk", [D, D], F32, kind="ExternalInput").ap()
    wv_d = nc.dram_tensor("Wv", [D, D], F32, kind="ExternalInput").ap()
    wq_d = nc.dram_tensor("Wq", [D, D], F32, kind="ExternalInput").ap()
    wo_d = nc.dram_tensor("Wout", [D, D], F32, kind="ExternalInput").ap()
    w0_d = nc.dram_tensor("W0", [D, D], F32, kind="ExternalInput").ap()
    w1_d = nc.dram_tensor("W1", [D, D], F32, kind="ExternalInput").ap()
    wgd_d = nc.dram_tensor("Wg_decay", [1, D], F32, kind="ExternalInput").ap()
    bgd_d = nc.dram_tensor("bg_decay", [1, 1], F32, kind="ExternalInput").ap()
    wgl_d = nc.dram_tensor("Wg_lr", [1, D], F32, kind="ExternalInput").ap()
    bgl_d = nc.dram_tensor("bg_lr", [1, 1], F32, kind="ExternalInput").ap()
    k0_d = nc.dram_tensor("k0", [2 * D, H], U8, kind="ExternalInput").ap()
    k1_d = nc.dram_tensor("k1", [H, H], U8, kind="ExternalInput").ap()
    k2_d = nc.dram_tensor("k2", [H, D], U8, kind="ExternalInput").ap()
    s0_d = nc.dram_tensor("s0", [1, 1], F32, kind="ExternalInput").ap()
    s1_d = nc.dram_tensor("s1", [1, 1], F32, kind="ExternalInput").ap()
    s2_d = nc.dram_tensor("s2", [1, 1], F32, kind="ExternalInput").ap()
    out_d = nc.dram_tensor("out", [R, D], F32, kind="ExternalOutput").ap()

    GN = 2 * DC * 128 * D + D
    gin_d = nc.dram_tensor("gbuf_in", [GN], F32).ap()
    gout_d = nc.dram_tensor("gbuf_out", [GN], F32).ap()

    def g1t_dr(buf, ec):
        return buf[ec * 128 * D:(ec + 1) * 128 * D].rearrange("(p d) -> p d", p=128)

    def g0t_dr(buf, ec):
        off = DC * 128 * D
        return buf[off + ec * 128 * D: off + (ec + 1) * 128 * D].rearrange(
            "(p d) -> p d", p=128)

    def colsum_dr(buf):
        off = 2 * DC * 128 * D
        return buf[off: off + D].rearrange("(p c) -> p c", p=128)

    dbg = {}
    if debug:
        for name, shape in [
            ("dbg_xn", [128, DC, TA]), ("dbg_k", [128, DC, TA]),
            ("dbg_a1", [128, DC, TA]), ("dbg_d2", [128, DC, TA]),
            ("dbg_d1", [128, DC, TA]), ("dbg_ktm", [128, RBA, D]),
            ("dbg_g1", [DC * 128, D]), ("dbg_g0", [DC * 128, D]),
            ("dbg_cs", [128, DC]), ("dbg_gate", [128, 8]),
            ("dbg_w0n", [128, DC, D]), ("dbg_w1n", [128, DC, D]),
            ("dbg_u1", [128, HC, TC]), ("dbg_u2", [128, HC, TC]),
            ("dbg_q", [128, DC, TC]), ("dbg_h", [128, DC, TC]),
            ("dbg_xa", [128, RBA, D]), ("dbg_stat", [128, 32]),
            ("dbg_xntm", [128, RBA, D]),
        ]:
            dbg[name] = nc.dram_tensor(name, shape, F32, kind="ExternalOutput").ap()

    slot = [nc.alloc_sbuf_tensor(f"wslot{i}", [128, DC, D], BF16) for i in range(5)]
    S_KT, S_VTN, S_0T, S_1N, S_1T = 0, 1, 2, 3, 4
    ident = nc.alloc_sbuf_tensor("ident", [128, 128], BF16)
    ones_r_f = nc.alloc_sbuf_tensor("ones_r_f", [1, 128], F32)
    ones_c_f = nc.alloc_sbuf_tensor("ones_c_f", [128, 1], F32)
    ones_r_b = nc.alloc_sbuf_tensor("ones_r_b", [1, 128], BF16)
    ones_c_b = nc.alloc_sbuf_tensor("ones_c_b", [128, 1], BF16)
    scal = nc.alloc_sbuf_tensor("scal", [128, 16], F32)
    C_1MA, C_NLR, C_NLRC, C_S0X2, C_S1X2, C_S2X2, C_BGD, C_BGL = range(8)
    sc11 = nc.alloc_sbuf_tensor("sc11", [1, 8], F32)
    colsum = nc.alloc_sbuf_tensor("colsum", [128, DC], F32)
    cs_tmp = nc.alloc_sbuf_tensor("cs_tmp", [128, DC], F32)
    wg_sb = nc.alloc_sbuf_tensor("wg_sb", [128, 2 * DC], F32)
    gate_sc = nc.alloc_sbuf_tensor("gate_sc", [128, 24], F32)
    stat = nc.alloc_sbuf_tensor("stat", [128, 32], F32)

    ARENA = 126 * 1024
    arena = nc.alloc_sbuf_tensor("arena", [128, ARENA], U8)
    base = nc.lookup_mloc(arena).addr

    def at(name, shape, dtype, off):
        nbytes = int(np.prod(shape[1:])) * mybir.dt.size(dtype)
        assert off % 32 == 0 and off + nbytes <= ARENA, (name, off, nbytes)
        return nc.alloc_sbuf_tensor_at(name, shape, dtype, offset=base + off)

    o = [0]
    def nxt(n):
        r = o[0]
        o[0] += (n + 31) // 32 * 32
        return r
    xA = at("xA", [128, RBA, D], F32, nxt(RBA * D * 4))
    xn_tmA = at("xn_tmA", [128, RBA, D], BF16, nxt(RBA * D * 2))
    xn_fmA = at("xn_fmA", [128, DC, TA], BF16, nxt(DC * TA * 2))
    p_pre1 = nxt(DC * TA * 4)
    pre1_sb = at("pre1_sb", [128, DC, TA], F32, p_pre1)
    k_fm = at("k_fm", [128, DC, TA], BF16, nxt(DC * TA * 2))
    a1_fm = at("a1_fm", [128, DC, TA], BF16, nxt(DC * TA * 2))
    d2_fm = at("d2_fm", [128, DC, TA], BF16, nxt(DC * TA * 2))
    d1_fm = at("d1_fm", [128, DC, TA], BF16, nxt(DC * TA * 2))
    deriv = at("deriv", [128, DC, TA], BF16, nxt(DC * TA * 2))
    k_tm = at("k_tm", [128, RBA, D], BF16, nxt(RBA * D * 2))
    a1_tm = at("a1_tm", [128, RBA, D], BF16, nxt(RBA * D * 2))
    d2_tm = at("d2_tm", [128, RBA, D], BF16, nxt(RBA * D * 2))
    d1_tm = at("d1_tm", [128, RBA, D], BF16, nxt(RBA * D * 2))
    assert o[0] <= ARENA, o[0]
    gstage = [at(f"gstage{i}", [128, D], F32, p_pre1 + i * D * 4) for i in range(4)]

    o = [0]
    xC = at("xC", [128, RBC, D], F32, nxt(RBC * D * 4))
    xn_tmC = at("xn_tmC", [128, RBC, D], BF16, nxt(RBC * D * 2))
    xn_fmC = at("xn_fmC", [128, DC, TC], BF16, nxt(DC * TC * 2))
    u1 = at("u1", [128, HC, TC], BF16, nxt(HC * TC * 2))
    p_u2 = nxt(HC * TC * 2)
    u2 = at("u2", [128, HC, TC], BF16, p_u2)
    q_fm = at("q_fm", [128, DC, TC], BF16, nxt(DC * TC * 2))
    s_fm = at("s_fm", [128, DC, TC], BF16, nxt(DC * TC * 2))
    w2s = [at(f"w2s{i}", [128, 2, 512], BF16, nxt(2 * 512 * 2)) for i in range(3)]
    k1u8 = at("k1u8", [128, 4, 512], U8, nxt(4 * 512))
    outst = [at(f"outst{i}", [128, 512], F32, nxt(512 * 4)) for i in range(2)]
    gtmp = at("gtmp", [128, 512], F32, nxt(512 * 4))
    assert o[0] <= ARENA, o[0]
    h_fm = q_fm
    k0s = [at(f"k0s{i}", [128, 2 * DC, 512], BF16, p_u2 + i * 2 * DC * 512 * 2)
           for i in range(2)]
    s3a = nc.lookup_mloc(slot[S_1N]).addr
    k1_s = [nc.alloc_sbuf_tensor_at(f"k1_s{i}", [128, 4, 512], BF16,
                                    offset=s3a + i * 4 * 512 * 2)
            for i in range(4)]
    k1u8b = nc.alloc_sbuf_tensor_at(
        "k1u8b", [128, 4, 512], U8, offset=nc.lookup_mloc(w2s[0]).addr)

    mm0 = nc.alloc_psum_tensor("mm0", [128, 512], F32)
    mm1 = nc.alloc_psum_tensor("mm1", [128, 512], F32)
    gA = nc.alloc_psum_tensor("gA", [128, 1024], F32)
    gB = nc.alloc_psum_tensor("gB", [128, 1024], F32)
    tp0 = nc.alloc_psum_tensor("tp0", [128, 1024], BF16)
    tp1 = nc.alloc_psum_tensor("tp1", [128, 1024], BF16)
    mmrot = [mm0, mm1]
    tprot = [tp0, tp1]

    sems = {n: nc.alloc_semaphore(name=n) for n in
            ["sx", "sg", "pe", "ve", "se", "gp", "cc",
             "ka0", "ka1", "ka2", "kb0", "kb1", "kb2", "kb3",
             "kc0", "kc1", "kc2", "kc3", "sw", "ku", "ku2"]}
    plan = {e: [] for e in ["tensor", "vector", "scalar", "gpsimd", "sync"]}
    cnt = {n: 0 for n in sems}
    lastwait = {(e, s): -1 for e in plan for s in sems}
    barred = {e: True for e in plan}
    bar_marks = {}

    def tile_barrier():
        for e in plan:
            barred[e] = False
        bar_marks.update({s: cnt[s] for s in ("pe", "ve", "se", "sg", "sx")})

    def emit(eng, fn, waits=(), inc=None, amt=1):
        w = list(waits)
        if not barred[eng]:
            barred[eng] = True
            if eng != "tensor":
                w = w + [(s, bar_marks[s]) for s in ("pe", "ve", "se", "sg", "sx")]
        ww = []
        for (s, v) in w:
            if v > lastwait[(eng, s)]:
                ww.append((s, v))
                lastwait[(eng, s)] = v
        plan[eng].append((ww, fn, inc, amt))
        if inc is not None:
            cnt[inc] += amt
            return cnt[inc]
        return None

    def dma(eng, dst, src, waits=(), sem=None, **kw):
        if sem is None:
            sem = "sg" if eng == "gpsimd" else "sx"
        return emit(eng, lambda e: e.dma_start(out=dst, in_=src, **kw),
                    waits=waits, inc=sem, amt=16)

    evt_free = {}

    def rfree(region):
        return [evt_free[region]] if region in evt_free else []

    def f_const(g):
        g.memset(ident[:, :], 0.0)
        g.affine_select(out=ident[:, :], in_=ident[:, :],
                        compare_op=ALU.not_equal, fill=1.0, base=0,
                        pattern=[[-1, 128]], channel_multiplier=1)
        g.memset(ones_r_f[:, :], 1.0)
        g.memset(ones_r_b[:, :], 1.0)
        g.memset(ones_c_b[:, :], 1.0)
        return g.memset(ones_c_f[:, :], 1.0)
    emit("gpsimd", f_const, inc="gp")

    sx_sc = []
    for i, sd in enumerate([s0_d, s1_d, s2_d, bgd_d, bgl_d]):
        sx_sc.append(dma("sync", sc11[:, i:i + 1], sd[:, :]))
    sx_sc.append(dma("sync", wg_sb[:, 0:DC],
                     wgd_d.rearrange("1 (c p) -> p c", p=128),
                     allow_slow_non_contiguous=True))
    sx_sc.append(dma("sync", wg_sb[:, DC:2 * DC],
                     wgl_d.rearrange("1 (c p) -> p c", p=128),
                     allow_slow_non_contiguous=True))
    sx_scal = sx_sc[-1]

    def f_init_v(v):
        v.memset(colsum[:, :], 0.0)
        return v.memset(stat[:, :], 0.0)
    ve_cs_chain = emit("vector", f_init_v, inc="ve")

    wnat = lambda wd: wd.rearrange("(c p) d -> p c d", p=128)

    def plan_weight_T(wd, dst_slot, scratch_slot, negate=False, sem="sg"):
        sgv = dma("gpsimd", slot[scratch_slot][:, :, :], wnat(wd),
                  waits=[("pe", cnt["pe"]), ("ve", cnt["ve"])], sem=sem)
        vev = 0
        for cd in range(DC):
            tname = f"tp{cd % 2}"
            tpb = tprot[cd % 2]

            def f_tp(t, cd=cd, tpb=tpb):
                last = None
                for cs in range(DC):
                    last = t.transpose(
                        tpb[:, cs * 128:(cs + 1) * 128],
                        slot[scratch_slot][:, cs, cd * 128:(cd + 1) * 128],
                        ident[:, :])
                return last
            pev = emit("tensor", f_tp,
                       waits=[(sem, sgv), ("gp", 1)] + rfree(tname), inc="pe")

            if negate:
                def f_ev(v, cd=cd, tpb=tpb):
                    return v.tensor_scalar(slot[dst_slot][:, cd, :], tpb[:, :],
                                           -1.0, None, ALU.mult)
            else:
                def f_ev(v, cd=cd, tpb=tpb):
                    return v.tensor_copy(slot[dst_slot][:, cd, :], tpb[:, :])
            vev = emit("vector", f_ev, waits=[("pe", pev)], inc="ve")
            evt_free[tname] = ("ve", vev)
        return sgv, vev

    _, ve_wk = plan_weight_T(wk_d, S_KT, S_1N)
    _, ve_wv = plan_weight_T(wv_d, S_VTN, S_1N, negate=True)
    _, ve_w0 = plan_weight_T(w0_d, S_0T, S_1N)
    sg_w1, ve_w1t = plan_weight_T(w1_d, S_1T, S_1N)

    def plan_ln_v2(xbuf, xn_tm, nrb, se1):
        def f_v2(v):
            for rb in range(nrb):
                v.tensor_tensor(stat[:, 28 + rb:29 + rb],
                                stat[:, 24 + rb:25 + rb],
                                stat[:, 24 + rb:25 + rb], ALU.mult)
            for rb in range(nrb):
                v.tensor_tensor(stat[:, 28 + rb:29 + rb],
                                stat[:, 16 + rb:17 + rb],
                                stat[:, 28 + rb:29 + rb], ALU.mult)
            for rb in range(nrb):
                v.tensor_scalar(stat[:, 28 + rb:29 + rb],
                                stat[:, 28 + rb:29 + rb], -0.5, 1.5,
                                ALU.mult, ALU.add)
            for rb in range(nrb):
                v.tensor_tensor(stat[:, 24 + rb:25 + rb],
                                stat[:, 24 + rb:25 + rb],
                                stat[:, 28 + rb:29 + rb], ALU.mult)
            for rb in range(nrb):
                v.tensor_scalar(xn_tm[:, rb, :], xbuf[:, rb, :],
                                stat[:, 12 + rb:13 + rb],
                                stat[:, 24 + rb:25 + rb], ALU.add, ALU.mult)
            return v.memset(stat[:, 0:8], 0.0)
        return emit("vector", f_v2, waits=[("se", se1)], inc="ve")

    def plan_ln_tile(xbuf, xn_tm, nrb, wx, defer_v2=False):
        assert nrb == 4
        def f_v1(v):
            for rb in range(nrb):
                v.reduce_sum(stat[:, rb:rb + 1], xbuf[:, rb, :], axis=AX.X)
            for rb in range(nrb):
                v.scalar_tensor_tensor(xn_tm[:, rb, :], xbuf[:, rb, :], 1.0,
                                       xbuf[:, rb, :], ALU.mult, ALU.mult,
                                       accum_out=stat[:, 4 + rb:5 + rb])
            for rb in range(nrb):
                v.tensor_scalar(stat[:, 8 + rb:9 + rb], stat[:, rb:rb + 1],
                                1.0 / D, None, ALU.mult)
            for rb in range(nrb):
                v.tensor_scalar(stat[:, 12 + rb:13 + rb], stat[:, rb:rb + 1],
                                -1.0 / D, None, ALU.mult)
            for rb in range(nrb):
                v.tensor_scalar(stat[:, 16 + rb:17 + rb],
                                stat[:, 4 + rb:5 + rb], 1.0 / D, 1e-5,
                                ALU.mult, ALU.add)
            for rb in range(nrb):
                v.tensor_tensor(stat[:, 20 + rb:21 + rb],
                                stat[:, 8 + rb:9 + rb],
                                stat[:, 8 + rb:9 + rb], ALU.mult)
            for rb in range(nrb):
                v.tensor_tensor(stat[:, 16 + rb:17 + rb],
                                stat[:, 16 + rb:17 + rb],
                                stat[:, 20 + rb:21 + rb], ALU.subtract)
            last = None
            for rb in range(nrb):
                last = v.reciprocal(stat[:, 24 + rb:25 + rb],
                                    stat[:, 16 + rb:17 + rb])
            return last
        ve1 = emit("vector", f_v1, waits=wx, inc="ve")
        se1 = emit("scalar",
                   lambda s: s.activation(stat[:, 24:28], stat[:, 24:28],
                                          AF.Sqrt),
                   waits=[("ve", ve1)], inc="se")
        if defer_v2:
            return se1

        def f_v2(v):
            for rb in range(nrb):
                v.tensor_tensor(stat[:, 28 + rb:29 + rb],
                                stat[:, 24 + rb:25 + rb],
                                stat[:, 24 + rb:25 + rb], ALU.mult)
            for rb in range(nrb):
                v.tensor_tensor(stat[:, 28 + rb:29 + rb],
                                stat[:, 16 + rb:17 + rb],
                                stat[:, 28 + rb:29 + rb], ALU.mult)
            for rb in range(nrb):
                v.tensor_scalar(stat[:, 28 + rb:29 + rb],
                                stat[:, 28 + rb:29 + rb], -0.5, 1.5,
                                ALU.mult, ALU.add)
            for rb in range(nrb):
                v.tensor_tensor(stat[:, 24 + rb:25 + rb],
                                stat[:, 24 + rb:25 + rb],
                                stat[:, 28 + rb:29 + rb], ALU.mult)
            for rb in range(nrb):
                v.tensor_scalar(xn_tm[:, rb, :], xbuf[:, rb, :],
                                stat[:, 12 + rb:13 + rb],
                                stat[:, 24 + rb:25 + rb], ALU.add, ALU.mult)
            return v.memset(stat[:, 0:8], 0.0)
        return emit("vector", f_v2, waits=[("se", se1)], inc="ve")

    def plan_transpose_fm(src_tm, dst_fm, rb, wsrc, eng="vector"):
        tname = f"tp{rb % 2}"
        tpb = tprot[rb % 2]

        def f_tp(t):
            last = None
            for dc in range(DC):
                last = t.transpose(tpb[:, dc * 128:(dc + 1) * 128],
                                   src_tm[:, dc * 128:(dc + 1) * 128], ident[:, :])
            return last
        pev = emit("tensor", f_tp, waits=list(wsrc) + rfree(tname), inc="pe")
        sem = "ve" if eng == "vector" else "se"

        def f_ev(v):
            if eng == "vector":
                return v.tensor_copy(
                    dst_fm[:, :, rb * 128:(rb + 1) * 128],
                    tpb[:, :].rearrange("p (c r) -> p c r", r=128))
            return v.copy(dst_fm[:, :, rb * 128:(rb + 1) * 128],
                          tpb[:, :].rearrange("p (c r) -> p c r", r=128))
        vev = emit(eng, f_ev, waits=[("pe", pev)], inc=sem)
        evt_free[tname] = (sem, vev)
        return (sem, vev)

    def plan_transpose_tm(src_fm, dst_tm, rb, wsrc):
        tname = f"tp{rb % 2}"
        tpb = tprot[rb % 2]

        def f_tp(t):
            last = None
            for dc in range(DC):
                last = t.transpose(tpb[:, dc * 128:(dc + 1) * 128],
                                   src_fm[:, dc, rb * 128:(rb + 1) * 128],
                                   ident[:, :])
            return last
        pev = emit("tensor", f_tp, waits=list(wsrc) + rfree(tname), inc="pe")
        vev = emit("vector", lambda v: v.tensor_copy(dst_tm[:, rb, :], tpb[:, :]),
                   waits=[("pe", pev)], inc="ve")
        evt_free[tname] = ("ve", vev)
        return vev

    def plan_proj(lhs_slot, lhs_col0, rhs_fm, rcols, reg_i, waits):
        reg = f"mm{reg_i}"

        def f_mm(t):
            last = None
            for ec in range(DC):
                last = t.matmul(mmrot[reg_i][:, :],
                                slot[lhs_slot][:, ec, lhs_col0:lhs_col0 + 128],
                                rhs_fm[:, ec, 0:rcols],
                                start=(ec == 0), stop=(ec == DC - 1))
            return last
        return emit("tensor", f_mm, waits=list(waits) + rfree(reg), inc="pe")

    def load_xA(t, wv):
        sxs = []
        xp = x_d.rearrange("(n p) d -> n p d", p=128)
        for rb in range(RBA):
            sxs.append(dma("sync", xA[:, rb, :], xp[t * RBA + rb], waits=wv))
        return sxs

    g1_first = [True] * DC
    g0_first = [True] * DC
    gdma_hist = []

    sxs0 = load_xA(0, [])
    ln_ev = plan_ln_tile(xA, xn_tmA, RBA, [("sx", sxs0[-1])])

    def emit_fm_block(t, ln_e, cs_guard):
        w = [("ve", ln_e)] + ([("ve", cs_guard)] if cs_guard else [])
        fm_ev = [plan_transpose_fm(xn_tmA[:, rb, :], xn_fmA, rb, w,
                                   eng="scalar") for rb in range(RBA)]
        return [fm_ev[-1]], cnt["pe"]

    def emit_k_block(xn_ready):
        k_ev = []
        for dc in range(DC):
            pev = plan_proj(S_KT, dc * 128, xn_fmA, TA, dc % 2,
                            xn_ready + [("ve", ve_wk)])
            vev = emit("vector",
                       lambda v, dc=dc: v.tensor_copy(k_fm[:, dc, :],
                                                      mmrot[dc % 2][:, :]),
                       waits=[("pe", pev)], inc="ve")
            evt_free[f"mm{dc % 2}"] = ("ve", vev)
            k_ev.append(vev)
        return [("ve", max(k_ev))]

    xn_ready_c, tp_pe_c = emit_fm_block(0, ln_ev, None)
    k_ready_c = emit_k_block(xn_ready_c)

    for t in range(NTA):
        tile_barrier()
        ve_ln = ln_ev
        ve_xn = [ve_ln] * RBA
        xn_ready = xn_ready_c
        k_ready = k_ready_c
        if debug and t == 0:
            dma("gpsimd", dbg["dbg_xa"][:, :, :], xA[:, :, :],
                waits=[("ve", ve_xn[-1])])
            dma("gpsimd", dbg["dbg_stat"][:, :], stat[:, :],
                waits=[("ve", ve_xn[-1])])
            dma("gpsimd", dbg["dbg_xntm"][:, :, :], xn_tmA[:, :, :],
                waits=[("ve", ve_xn[-1])])

        ktm_ev = [plan_transpose_tm(k_fm, k_tm, rb, k_ready) for rb in range(RBA)]

        p1_ev = []
        for dc in range(DC):
            pev = plan_proj(S_0T, dc * 128, k_fm, TA, dc % 2,
                            k_ready + [("ve", ve_w0)])
            vev = emit("vector",
                       lambda v, dc=dc: v.tensor_copy(pre1_sb[:, dc, :],
                                                      mmrot[dc % 2][:, :]),
                       waits=[("pe", pev)], inc="ve")
            evt_free[f"mm{dc % 2}"] = ("ve", vev)
            p1_ev.append(vev)
        p1_ready = [("ve", max(p1_ev))]

        se_a1 = 0
        for dc in range(DC):
            se_a1 = emit("scalar",
                         lambda s, dc=dc: s.activation(
                             a1_fm[:, dc, :], pre1_sb[:, dc, :], AF.Silu),
                         waits=[("ve", p1_ev[dc])], inc="se")
        def f_dsilu(s):
            last = None
            for dc in range(DC):
                last = s.activation(deriv[:, dc, :], pre1_sb[:, dc, :],
                                    AF.Derivative_silu)
            return last
        se_dv = emit("scalar", f_dsilu, waits=[("se", se_a1)], inc="se")
        a1_ready = [("se", se_a1)]
        a1tm_ev = [plan_transpose_tm(a1_fm, a1_tm, rb, a1_ready)
                   for rb in range(RBA)]

        d2_ev = []
        for dc in range(DC):
            reg = f"mm{dc % 2}"

            def f_mm(t_, dc=dc):
                last = None
                for ec in range(DC):
                    t_.matmul(mmrot[dc % 2][:, :],
                              slot[S_1T][:, ec, dc * 128:(dc + 1) * 128],
                              a1_fm[:, ec, :], start=(ec == 0), stop=False)
                for ec in range(DC):
                    last = t_.matmul(mmrot[dc % 2][:, :],
                                     slot[S_VTN][:, ec, dc * 128:(dc + 1) * 128],
                                     xn_fmA[:, ec, :], start=False,
                                     stop=(ec == DC - 1))
                return last
            pev = emit("tensor", f_mm,
                       waits=a1_ready + xn_ready
                       + [("ve", ve_wv), ("ve", ve_w1t)] + rfree(reg), inc="pe")
            vev = emit("vector",
                       lambda v, dc=dc: v.tensor_scalar(
                           d2_fm[:, dc, :], mmrot[dc % 2][:, :],
                           5.0, -5.0, ALU.min, ALU.max),
                       waits=[("pe", pev)], inc="ve")
            evt_free[reg] = ("ve", vev)
            d2_ev.append(vev)
        d2_ready = [("ve", max(d2_ev))]

        def f_cs(v):
            for dc in range(DC):
                v.reduce_sum(cs_tmp[:, dc:dc + 1], xn_fmA[:, dc, :], axis=AX.X)
            v.tensor_tensor(colsum[:, 0:4], colsum[:, 0:4], cs_tmp[:, 0:4],
                            ALU.add)
            return v.tensor_tensor(colsum[:, 4:8], colsum[:, 4:8],
                                   cs_tmp[:, 4:8], ALU.add)
        ve_cs_chain = emit("vector", f_cs,
                           waits=xn_ready + [("ve", ve_cs_chain)], inc="ve")

        d2tm_ev = [plan_transpose_tm(d2_fm, d2_tm, rb, d2_ready)
                   for rb in range(RBA)]

        if t + 1 < NTA:
            sxs = load_xA(t + 1, [("ve", ve_ln)])
            ln_ev = plan_ln_tile(xA, xn_tmA, RBA,
                                 [("sx", sxs[-1]), ("pe", tp_pe_c)])
            xn_ready_c, tp_pe_c = emit_fm_block(t + 1, ln_ev, ve_cs_chain)
            k_ready_c = emit_k_block(xn_ready_c)

        d1_ev = []
        for dc in range(DC):
            pev = plan_proj(S_1N, dc * 128, d2_fm, TA, dc % 2,
                            d2_ready + [("sg", sg_w1), ("se", se_dv)])
            vev = emit("vector",
                       lambda v, dc=dc: v.scalar_tensor_tensor(
                           d1_fm[:, dc, :], mmrot[dc % 2][:, :], CSCALE,
                           deriv[:, dc, :], ALU.mult, ALU.mult),
                       waits=[("pe", pev)], inc="ve")
            evt_free[f"mm{dc % 2}"] = ("ve", vev)
            d1_ev.append(vev)
        d1_ready = [("ve", max(d1_ev))]
        d1tm_ev = [plan_transpose_tm(d1_fm, d1_tm, rb, d1_ready)
                   for rb in range(RBA)]

        a1tm_r = [("ve", max(a1tm_ev))]
        d2tm_r = [("ve", max(d2tm_ev))]
        ktm_r = [("ve", max(ktm_ev))]
        d1tm_r = [("ve", max(d1tm_ev))]

        def plan_g(ec, lhs_tm, rhs_tm, dram_fn, first_flags, wop):
            gname = "gA" if ec % 2 == 0 else "gB"
            gps = gA if ec % 2 == 0 else gB

            def f_mm(t_):
                last = None
                for hf in range(2):
                    for rb in range(RBA):
                        last = t_.matmul(
                            gps[:, hf * 512:(hf + 1) * 512],
                            lhs_tm[:, rb, ec * 128:(ec + 1) * 128],
                            rhs_tm[:, rb, hf * 512:(hf + 1) * 512],
                            start=(rb == 0), stop=(rb == RBA - 1))
                return last
            pev = emit("tensor", f_mm, waits=wop + rfree(gname), inc="pe")
            gst = gstage[plan_g.gidx % 4]
            plan_g.gidx += 1
            gw = [("pe", pev), ("se", se_dv)]
            if len(gdma_hist) >= 4:
                gw.append(("sg", gdma_hist[-4]))
            sev = emit("scalar", lambda s: s.copy(gst[:, :], gps[:, :]),
                       waits=gw, inc="se")
            evt_free[gname] = ("se", sev)
            kw = {} if first_flags[ec] else {"accum_op": ALU.add}
            first_flags[ec] = False
            sgv = dma("gpsimd", dram_fn(gin_d, ec), gst[:, :],
                      waits=[("se", sev)], **kw)
            gdma_hist.append(sgv)

        if not hasattr(plan_g, "gidx"):
            plan_g.gidx = 0
        for ec in range(DC):
            plan_g(ec, a1_tm, d2_tm, g1t_dr, g1_first, a1tm_r + d2tm_r)
        for ec in range(DC):
            plan_g(ec, k_tm, d1_tm, g0t_dr, g0_first, ktm_r + d1tm_r)

        if debug and t == 0:
            for nm, buf in [("dbg_xn", xn_fmA), ("dbg_k", k_fm),
                            ("dbg_a1", a1_fm), ("dbg_d2", d2_fm),
                            ("dbg_d1", d1_fm)]:
                dma("gpsimd", dbg[nm][:, :, :], buf[:, :, :],
                    waits=[("se", cnt["se"]), ("ve", cnt["ve"]),
                           ("pe", cnt["pe"])])
            dma("gpsimd", dbg["dbg_ktm"][:, :, :], k_tm[:, :, :])

    dma("gpsimd", colsum_dr(gin_d), colsum[:, :],
        waits=[("ve", ve_cs_chain)])
    sg_all_g = cnt["sg"]
    pe_endA = cnt["pe"]

    tile_barrier()
    _, ve_wq = plan_weight_T(wq_d, S_KT, S_1N, sem="sw")
    _, ve_wo = plan_weight_T(wo_d, S_VTN, S_1N, sem="sw")
    S_QT, S_OT = S_KT, S_VTN

    def plan_bcast_f(srcap, dstcol, mult, waits=()):
        pev = emit("tensor",
                   lambda t: t.matmul(mm0[:, 0:1], ones_r_f[:, :], srcap,
                                      start=True, stop=True),
                   waits=[("gp", 1)] + list(waits) + rfree("mm0"), inc="pe")
        vev = emit("vector",
                   lambda v: v.tensor_scalar(scal[:, dstcol:dstcol + 1],
                                             mm0[:, 0:1], mult, None, ALU.mult),
                   waits=[("pe", pev)], inc="ve")
        evt_free["mm0"] = ("ve", vev)
        return vev

    ve_s0 = plan_bcast_f(sc11[:, 0:1], C_S0X2, 2.0, waits=[("sx", sx_sc[0])])
    ve_s1 = plan_bcast_f(sc11[:, 1:2], C_S1X2, 2.0, waits=[("sx", sx_sc[1])])
    ve_s2a = plan_bcast_f(sc11[:, 2:3], C_S2X2, 2.0, waits=[("sx", sx_sc[2])])
    ve_bgd = plan_bcast_f(sc11[:, 3:4], C_BGD, 1.0, waits=[("sx", sx_sc[3])])
    ve_bgl = plan_bcast_f(sc11[:, 4:5], C_BGL, 1.0, waits=[("sx", sx_sc[4])])

    S_0NT, S_1NT = S_0T, S_1T
    hold = {}

    def plan_gates_and_wn():
        sx_csr = dma("sync", cs_tmp[:, :], colsum_dr(gout_d), waits=[("cc", 1)])

        def f_gdot(v):
            v.tensor_tensor(gate_sc[:, 8:8 + DC], cs_tmp[:, :], wg_sb[:, 0:DC],
                            ALU.mult)
            v.tensor_tensor(gate_sc[:, 16:16 + DC], cs_tmp[:, :],
                            wg_sb[:, DC:2 * DC], ALU.mult)
            v.reduce_sum(gate_sc[:, 0:1], gate_sc[:, 8:8 + DC], axis=AX.X)
            return v.reduce_sum(gate_sc[:, 1:2], gate_sc[:, 16:16 + DC],
                                axis=AX.X)
        ve_gd = emit("vector", f_gdot,
                     waits=[("sx", sx_csr), ("sx", sx_scal), ("pe", pe_endA)],
                     inc="ve")

        def plan_gate(src_col, dst_col, bias_col):
            pev = emit("tensor",
                       lambda t: t.matmul(mm1[0:1, 0:1], ones_c_f[:, :],
                                          stat[:, src_col:src_col + 1],
                                          start=True, stop=True),
                       waits=[("ve", ve_gd), ("gp", 1)] + rfree("mm1"), inc="pe")
            vev = emit("vector",
                       lambda v: v.tensor_copy(gate_sc[0:1, dst_col:dst_col + 1],
                                               mm1[0:1, 0:1]),
                       waits=[("pe", pev)], inc="ve")
            evt_free["mm1"] = ("ve", vev)
            pev2 = emit("tensor",
                        lambda t: t.matmul(mm1[:, 0:1], ones_r_f[:, :],
                                           gate_sc[0:1, dst_col:dst_col + 1],
                                           start=True, stop=True),
                        waits=[("ve", vev)] + rfree("mm1"), inc="pe")
            vev2 = emit("vector",
                        lambda v: v.scalar_tensor_tensor(
                            gate_sc[:, dst_col:dst_col + 1], mm1[:, 0:1],
                            1.0 / 32768.0,
                            scal[:, (C_BGD if bias_col == 0 else C_BGL):
                                 (C_BGD if bias_col == 0 else C_BGL) + 1],
                            ALU.mult, ALU.add),
                        waits=[("pe", pev2), ("ve", ve_bgd), ("ve", ve_bgl)],
                        inc="ve")
            evt_free["mm1"] = ("ve", vev2)
            return vev2

        ve_zd = plan_gate(0, 2, 0)
        ve_zl = plan_gate(1, 3, 1)

        def f_sig(s):
            s.activation(gate_sc[:, 2:3], gate_sc[:, 2:3], AF.Sigmoid)
            return s.activation(gate_sc[:, 3:4], gate_sc[:, 3:4], AF.Sigmoid)
        se_sig = emit("scalar", f_sig, waits=[("ve", ve_zd), ("ve", ve_zl)], inc="se")

        def f_gder(v):
            v.tensor_scalar(scal[:, C_1MA:C_1MA + 1], gate_sc[:, 2:3], -1.0, 1.0,
                            ALU.mult, ALU.add)
            v.tensor_scalar(scal[:, C_NLR:C_NLR + 1], gate_sc[:, 3:4], -1.0, None,
                            ALU.mult)
            return v.tensor_scalar(scal[:, C_NLRC:C_NLRC + 1], gate_sc[:, 3:4],
                                   -CSCALE, None, ALU.mult)
        ve_gder = emit("vector", f_gder, waits=[("se", se_sig)], inc="ve")

        if debug:
            dma("gpsimd", dbg["dbg_cs"][:, :], cs_tmp[:, :], waits=[("sx", sx_csr)])
            dma("gpsimd", dbg["dbg_gate"][:, 0:8], gate_sc[:, 0:8],
                waits=[("ve", ve_gder)])

        wn_last = ve_gder
        first_form = True
        gld = [gtmp, outst[1]]
        form_prev = [wn_last, wn_last]
        it = 0
        for slot_i, dram_fn, clip_v, nlr_col in [
                (S_0T, g0t_dr, 1.0, C_NLR), (S_1T, g1t_dr, CLIP_HI, C_NLRC)]:
            for ec in range(DC):
                for hf in range(2):
                    gb = gld[it % 2]
                    src = dram_fn(gout_d, ec)[:, hf * 512:(hf + 1) * 512]
                    sxv = dma("sync", gb[:, :], src,
                              waits=[("cc", 1), ("ve", form_prev[it % 2])])

                    def f_form(v, slot_i=slot_i, ec=ec, hf=hf, clip_v=clip_v,
                               nlr_col=nlr_col, gb=gb):
                        sl = slot[slot_i][:, ec, hf * 512:(hf + 1) * 512]
                        v.tensor_scalar(outst[0][:, :], gb[:, :], clip_v,
                                        -clip_v, ALU.min, ALU.max)
                        v.tensor_scalar(gb[:, :], sl,
                                        scal[:, C_1MA:C_1MA + 1], None,
                                        ALU.mult)
                        return v.scalar_tensor_tensor(
                            sl, outst[0][:, :], scal[:, nlr_col:nlr_col + 1],
                            gb[:, :], ALU.mult, ALU.add)
                    fw = [("sx", sxv)]
                    if first_form:
                        fw.append(("pe", pe_endA))
                        first_form = False
                    wn_last = emit("vector", f_form, waits=fw, inc="ve")
                    form_prev[it % 2] = wn_last
                    it += 1
        ve_wn = wn_last

        if debug:
            dma("gpsimd", dbg["dbg_w0n"][:, :, :], slot[S_0NT][:, :, :],
                waits=[("ve", ve_wn)])
            dma("gpsimd", dbg["dbg_w1n"][:, :, :], slot[S_1NT][:, :, :],
                waits=[("ve", ve_wn)])
        hold["ve_wn"] = ve_wn


    xp = x_d.rearrange("(n p) d -> n p d", p=128)
    k0p = k0_d.rearrange("(c p) j -> p c j", p=128)
    k1p = k1_d.rearrange("(c p) j -> p c j", p=128)
    k2p = k2_d.rearrange("(c p) d -> p c d", p=128)
    outp = out_d.rearrange("(n p) d -> p n d", p=128) if False else \
        out_d.rearrange("(n p) d -> n p d", p=128)

    f3r = [("mm0", mm0[:, :]), ("mm1", mm1[:, :]),
           ("gA0", gA[:, 0:512]), ("gA1", gA[:, 512:1024])]
    f2r = f3r + [("gB0", gB[:, 0:512]), ("gB1", gB[:, 512:1024])]
    if "gA" in evt_free:
        evt_free["gA0"] = evt_free["gA1"] = evt_free["gA"]
    if "gB" in evt_free:
        evt_free["gB0"] = evt_free["gB1"] = evt_free["gB"]
    k0_reader = [0, 0, 0]
    k1_reader = [0, 0, 0, 0]
    w2_reader = [pe_endA, pe_endA, pe_endA]
    out_dma = [0, 0]

    def tile_part1(t, ln_ev, sync_k1=False):
        fm_ev = [plan_transpose_fm(xn_tmC[:, rb, :], xn_fmC, rb,
                                   [("ve", ln_ev)], eng="scalar")
                 for rb in range(RBC)]
        xn_ready = [fm_ev[-1]]
        tp_pe = cnt["pe"]

        q_ev = []
        for dc in range(DC):
            pev = plan_proj(S_QT, dc * 128, xn_fmC, TC, dc % 2,
                            xn_ready + [("ve", ve_wq)])
            vev = emit("vector",
                       lambda v, dc=dc: v.tensor_copy(q_fm[:, dc, :],
                                                      mmrot[dc % 2][:, :]),
                       waits=[("pe", pev)], inc="ve")
            evt_free[f"mm{dc % 2}"] = ("ve", vev)
            q_ev.append(vev)
        q_ready = [("ve", max(q_ev))]

        for jg in range(8):
            bi = jg % 2
            k0w = [("pe", k0_reader[bi])] if k0_reader[bi] else \
                [("ve", ve_wo), ("pe", pe_endA)]
            sgb = dma("gpsimd", k0s[bi][:, :, :],
                      k0p[:, :, jg * 512:(jg + 1) * 512], waits=k0w,
                      sem=f"ka{bi}")
            ve_w = emit("vector",
                        lambda v, bi=bi: v.tensor_tensor(
                            k0s[bi][:, 0:DC, :], k0s[bi][:, 0:DC, :],
                            k0s[bi][:, DC:2 * DC, :], ALU.subtract),
                        waits=[(f"ka{bi}", sgb)], inc="ve")
            for jj in range(4):
                jc = jg * 4 + jj
                reg = f"mm{jj % 2}"

                def f_mm(t_, jj=jj, bi=bi):
                    last = None
                    for ec in range(DC):
                        last = t_.matmul(mmrot[jj % 2][:, :],
                                         k0s[bi][:, ec, jj * 128:(jj + 1) * 128],
                                         xn_fmC[:, ec, :],
                                         start=(ec == 0), stop=(ec == DC - 1))
                    return last
                pev = emit("tensor", f_mm,
                           waits=xn_ready + [("ve", ve_w)] + rfree(reg),
                           inc="pe")
                sev = emit("scalar",
                           lambda s, jc=jc, jj=jj: s.activation(
                               u1[:, jc, :], mmrot[jj % 2][:, :], AF.Relu,
                               scale=scal[:, C_S0X2:C_S0X2 + 1]),
                           waits=[("pe", pev), ("ve", ve_s0)], inc="se")
                evt_free[reg] = ("se", sev)
            k0_reader[bi] = cnt["pe"]
        u1_ready = [("se", cnt["se"])]

        for jg in range(8):
            jcs = [jg * 4 + jj for jj in range(4)]
            regs = [f2r[jc % 6] for jc in jcs]
            pev = 0
            for eo8 in range(8):
                bi = eo8 % 4
                k1w = [("pe", k1_reader[bi])] if k1_reader[bi] else \
                    ([("ve", ve_wo), ("pe", pe_endA)] + u1_ready)
                if sync_k1:
                    gi = jg * 8 + eo8
                    bu = [k1u8, k1u8b][gi % 2]
                    ksem = ["ku", "ku2"][gi % 2]
                    prev_cv = tile_part1.k1u8_rd[gi % 2]
                    ldw = [("ve", prev_cv)] if prev_cv else list(k1w)
                    sxv = dma("sync", bu[:, :, :],
                              k1p[:, eo8 * 4:(eo8 + 1) * 4,
                                  jg * 512:(jg + 1) * 512], waits=ldw,
                              sem=ksem)
                    ve_cv = emit("vector",
                                 lambda v, bi=bi, bu=bu: v.tensor_scalar(
                                     k1_s[bi][:, :, :], bu[:, :, :],
                                     2.0, -1.0, ALU.mult, ALU.add),
                                 waits=[(ksem, sxv)] + k1w, inc="ve")
                    tile_part1.k1u8_rd[gi % 2] = ve_cv
                else:
                    sgv = dma("gpsimd", k1_s[bi][:, :, :],
                              k1p[:, eo8 * 4:(eo8 + 1) * 4,
                                  jg * 512:(jg + 1) * 512], waits=k1w,
                              sem=f"kb{bi}")
                    ve_cv = emit("vector",
                                 lambda v, bi=bi: v.tensor_scalar(
                                     k1_s[bi][:, :, :], k1_s[bi][:, :, :],
                                     2.0, -1.0, ALU.mult, ALU.add),
                                 waits=[(f"kb{bi}", sgv)], inc="ve")

                def f_mm(t_, eo8=eo8, bi=bi, regs=regs):
                    last = None
                    for jj in range(4):
                        for ecq in range(4):
                            last = t_.matmul(
                                regs[jj][1],
                                k1_s[bi][:, ecq, jj * 128:(jj + 1) * 128],
                                u1[:, eo8 * 4 + ecq, :],
                                start=(eo8 == 0 and ecq == 0),
                                stop=(eo8 == 7 and ecq == 3))
                    return last
                mw = [("ve", ve_cv)] + u1_ready
                if eo8 == 0:
                    for nm, _ in regs:
                        mw += rfree(nm)
                pev = emit("tensor", f_mm, waits=mw, inc="pe")
                k1_reader[bi] = cnt["pe"]
            for jj, jc in enumerate(jcs):
                sev = emit("scalar",
                           lambda s, jc=jc, jj=jj, regs=regs: s.activation(
                               u2[:, jc, :], regs[jj][1], AF.Relu,
                               scale=scal[:, C_S1X2:C_S1X2 + 1]),
                           waits=[("pe", pev), ("ve", ve_s1)], inc="se")
                evt_free[regs[jj][0]] = ("se", sev)
        ve_u2s = emit("vector",
                      lambda v: v.tensor_scalar(u2[:, :, :], u2[:, :, :],
                                                scal[:, C_S2X2:C_S2X2 + 1],
                                                None, ALU.mult),
                      waits=[("se", cnt["se"]), ("ve", ve_s2a)], inc="ve")
        u2_ready = [("se", cnt["se"]), ("ve", ve_u2s)]
        return dict(q_ready=q_ready, u1_ready=u1_ready,
                    u2_ready=u2_ready, tp_pe=tp_pe, ln_ev=ln_ev)

    def tile_part2(t, st):
        q_ready, u2_ready = st["q_ready"], st["u2_ready"]
        s_ev = []
        for dc in range(DC):
            pev = plan_proj(S_0NT, dc * 128, q_fm, TC, dc % 2,
                            q_ready + [("ve", hold["ve_wn"])])
            sev = emit("scalar",
                       lambda s, dc=dc: s.activation(s_fm[:, dc, :],
                                                     mmrot[dc % 2][:, :],
                                                     AF.Silu),
                       waits=[("pe", pev)], inc="se")
            evt_free[f"mm{dc % 2}"] = ("se", sev)
            s_ev.append(sev)
        s_ready = [("se", max(s_ev))]
        h_ev = []
        for dc in range(DC):
            pev = plan_proj(S_1NT, dc * 128, s_fm, TC, dc % 2, s_ready)
            vev = emit("vector",
                       lambda v, dc=dc: v.tensor_copy(h_fm[:, dc, :],
                                                      mmrot[dc % 2][:, :]),
                       waits=[("pe", pev)] + q_ready, inc="ve")
            evt_free[f"mm{dc % 2}"] = ("ve", vev)
            h_ev.append(vev)
        h_ready = [("ve", max(h_ev))]

        ln_se = None
        if t + 1 < NTC:
            sxs = []
            for rb in range(RBC):
                sxs.append(dma("sync", xC[:, rb, :], xp[(t + 1) * RBC + rb],
                               waits=[("ve", st["ln_ev"])]))
            ln_se = plan_ln_tile(xC, xn_tmC, RBC,
                                 [("sx", sxs[-1]), ("pe", st["tp_pe"])],
                                 defer_v2=True)

        if debug and t == 0:
            dma("gpsimd", dbg["dbg_q"][:, :, :], q_fm[:, :, :],
                waits=q_ready + [("pe", cnt["pe"])])
            dma("gpsimd", dbg["dbg_u1"][:, :, :], u1[:, :, :],
                waits=st["u1_ready"])
            dma("gpsimd", dbg["dbg_u2"][:, :, :], u2[:, :, :], waits=u2_ready)
            dma("gpsimd", dbg["dbg_h"][:, :, :], h_fm[:, :, :], waits=h_ready)

        ln_v2 = [None]
        for hf in range(2):
            if hf == 1 and ln_se is not None:
                ln_v2[0] = plan_ln_v2(xC, xn_tmC, RBC, ln_se)
            pev = 0
            for grp in range(16):
                bi = grp % 3
                w2w = [("pe", w2_reader[bi])]
                if t == 0 and hf == 0 and grp == 0:
                    w2w += [("ve", max(tile_part1.k1u8_rd))]
                sgv = dma("gpsimd", w2s[bi][:, :, :],
                          k2p[:, grp * 2:(grp + 1) * 2,
                              hf * 512:(hf + 1) * 512], waits=w2w,
                          sem=f"kc{bi}")
                ve_cv = emit("vector",
                             lambda v, bi=bi: v.tensor_scalar(
                                 w2s[bi][:, :, :], w2s[bi][:, :, :],
                                 2.0, -1.0, ALU.mult, ALU.add),
                             waits=[(f"kc{bi}", sgv)], inc="ve")

                def f_mm(t_, grp=grp, bi=bi):
                    last = None
                    for rb in range(RBC):
                        for jj in range(2):
                            jc = grp * 2 + jj
                            last = t_.matmul(
                                f3r[rb][1],
                                u2[:, jc, rb * 128:(rb + 1) * 128],
                                w2s[bi][:, jj, :],
                                start=(jc == 0), stop=False)
                    return last
                mw = [("ve", ve_cv)] + u2_ready
                if grp == 0:
                    for nm, _ in f3r:
                        mw += rfree(nm)
                pev = emit("tensor", f_mm, waits=mw, inc="pe")
                w2_reader[bi] = cnt["pe"]

            def f_ao(t_, hf=hf):
                last = None
                for rb in range(RBC):
                    for ec in range(DC):
                        last = t_.matmul(
                            f3r[rb][1],
                            h_fm[:, ec, rb * 128:(rb + 1) * 128],
                            slot[S_OT][:, ec, hf * 512:(hf + 1) * 512],
                            start=False, stop=(ec == DC - 1))
                return last
            pev = emit("tensor", f_ao, waits=h_ready + [("ve", ve_wo)],
                       inc="pe")

            for rb in range(RBC):
                ob = outst[rb % 2]
                lw = []
                if out_dma[rb % 2]:
                    lw.append(("sx", out_dma[rb % 2]))
                sxr = dma("sync", ob[:, :],
                          xp[t * RBC + rb][:, hf * 512:(hf + 1) * 512],
                          waits=lw)
                vev = emit("vector",
                           lambda v, rb=rb, ob=ob: v.tensor_tensor(
                               ob[:, :], f3r[rb][1], ob[:, :], ALU.add),
                           waits=[("pe", pev), ("sx", sxr)], inc="ve")
                evt_free[f3r[rb][0]] = ("ve", vev)
                out_dma[rb % 2] = dma(
                    "sync", outp[t * RBC + rb][:, hf * 512:(hf + 1) * 512],
                    ob[:, :], waits=[("ve", vev)])
        if ln_se is None:
            return None
        return ln_v2[0]

    sxs0c = []
    for rb in range(RBC):
        sxs0c.append(dma("sync", xC[:, rb, :], xp[rb],
                         waits=[("ve", ln_ev)]))
    ln0c = plan_ln_tile(xC, xn_tmC, RBC, [("sx", sxs0c[-1]), ("pe", pe_endA)])
    tile_part1.k1u8_rd = [0, 0]
    st0 = tile_part1(0, ln0c, sync_k1=True)
    emit("gpsimd",
         lambda g: g.collective_compute(
             "AllReduce", ALU.add, replica_groups=[list(range(NCORES))],
             ins=[gin_d.opt()], outs=[gout_d.opt()]),
         waits=[("sg", sg_all_g)], inc="cc")
    plan_gates_and_wn()
    if debug:
        dma("gpsimd", dbg["dbg_g1"][:, :],
            gout_d[0:DC * 128 * D].rearrange("(p d) -> p d", p=DC * 128),
            waits=[("cc", 1)])
        dma("gpsimd", dbg["dbg_g0"][:, :],
            gout_d[DC * 128 * D:2 * DC * 128 * D].rearrange(
                "(p d) -> p d", p=DC * 128),
            waits=[("cc", 1)])
    ln_ev_c = tile_part2(0, st0)
    for t in range(1, NTC):
        tile_barrier()
        st = tile_part1(t, ln_ev_c)
        ln_ev_c = tile_part2(t, st)

    sx_fin, sg_fin = cnt["sx"], cnt["sg"]
    emit("sync", lambda e: e.nop(), waits=[("sx", sx_fin)])
    fin_waits = [("sg", sg_fin), ("sx", sx_fin)] +         [(s, cnt[s]) for s in ["ka0", "ka1", "kb0", "kb1", "kc0", "kc1"]]
    emit("gpsimd", lambda e: e.nop(), waits=fin_waits)

    with nc.Block() as block:
        def runner(eng_name):
            def body(e):
                for w, fn, inc, amt in plan[eng_name]:
                    for (s, v) in w:
                        e.wait_ge(sems[s], v)
                    inst = fn(e)
                    if inc is not None and inst is not None:
                        inst.then_inc(sems[inc], amt)
            return body
        block.tensor(runner("tensor"))
        block.vector(runner("vector"))
        block.scalar(runner("scalar"))
        block.gpsimd(runner("gpsimd"))
        block.sync(runner("sync"))

    return nc


_NC_CACHE = {}


def kernel(**inputs):
    x = np.ascontiguousarray(np.asarray(inputs["L0_attn_x"], dtype=np.float32))
    f32 = lambda k: np.ascontiguousarray(np.asarray(inputs[k], dtype=np.float32))
    b2u = lambda k: np.ascontiguousarray(
        np.asarray(inputs[k]).astype(np.uint8, copy=False))
    common = {
        "Wk": f32("L0_attn_Wk"), "Wv": f32("L0_attn_Wv"),
        "Wq": f32("L0_attn_Wq"), "Wout": f32("L0_attn_Wout"),
        "W0": f32("L0_attn_W0"), "W1": f32("L0_attn_W1"),
        "Wg_decay": f32("L0_attn_Wg_decay").reshape(1, D),
        "bg_decay": f32("L0_attn_bg_decay").reshape(1, 1),
        "Wg_lr": f32("L0_attn_Wg_lr").reshape(1, D),
        "bg_lr": f32("L0_attn_bg_lr").reshape(1, 1),
        "k0": b2u("L0_ffn_k0"), "k1": b2u("L0_ffn_k1"), "k2": b2u("L0_ffn_k2"),
        "s0": f32("L0_ffn_s0").reshape(1, 1),
        "s1": f32("L0_ffn_s1").reshape(1, 1),
        "s2": f32("L0_ffn_s2").reshape(1, 1),
    }
    if "nc" not in _NC_CACHE:
        _NC_CACHE["nc"] = build_nc()
    nc = _NC_CACHE["nc"]
    in_maps = [dict(common, x=np.ascontiguousarray(x[i])) for i in range(NCORES)]
    res = run_bass_kernel_spmd(nc, in_maps, core_ids=list(range(NCORES)))
    return np.stack([np.asarray(res.results[i]["out"]) for i in range(NCORES)], 0)
